# version 05
# baseline (speedup 1.0000x reference)
import contextlib
import numpy as np
import ml_dtypes
import concourse.bass as bass
import concourse.mybir as mybir
from concourse.bass_utils import run_bass_kernel_spmd

F32 = mybir.dt.float32
BF16 = mybir.dt.bfloat16
AF = mybir.ActivationFunctionType
ALU = mybir.AluOpType
AX = mybir.AxisListType

NCORE = 8
TOK = 1024
D = 4096
NH = 32
DK = 128
NE = 8
DC = 32
TS = 8
INV_SQRT_DK = 0.08838834764831845
EPS = 1e-5

DEBUG_DUMPS = []
_DBG_SPECS = {
    "xnT": ([128, DC * TOK], BF16),
    "k_own": ([2048, TOK], BF16),
    "q_own": ([D, TOK], BF16),
    "kg0": ([4 * 2049, TOK], BF16),
    "vg0": ([4 * 16 * TOK, DK], BF16),
    "o_fm": ([128, DC * TOK], BF16),
    "x2": ([D, TOK], BF16),
    "xn2T": ([128, DC * TOK], BF16),
    "zT": ([128, DC * TOK], BF16),
    "r8": ([NE, TOK], BF16),
    "et8": ([NE, TOK], F32),
    "rsrep": ([128, TOK], F32),
    "bias_cols": ([128, DC], F32),
    "osc_last": ([128, TOK], BF16),
    "tch_last": ([128, TOK], F32),
}


class Sem:
    def __init__(self, h):
        self.h = h
        self.n = 0

    def inc(self, ins, k=1):
        ins.then_inc(self.h, k)
        self.n += k
        return ins


def _blocked_T(w):
    return np.ascontiguousarray(w.reshape(DC, 128, DC, 128).transpose(0, 3, 2, 1))


class Builder:
    def __init__(self, debug_dumps=(), phases=5):
        self.phases = phases
        self.debug_dumps = list(debug_dumps)
        nc = bass.Bass("TRN2", target_bir_lowering=False)
        self.nc = nc
        self.x_t = nc.dram_tensor("x_t", [TOK, D], F32, kind="ExternalInput").ap()
        self.w_blk = {}
        for nm in ("wq", "wk", "wv", "wo", "wf"):
            self.w_blk[nm] = nc.dram_tensor(nm + "_blk", [DC, 128, DC, 128], F32,
                                            kind="ExternalInput").ap()
        self.gwt_h = nc.dram_tensor("gwt_h", [128, DC * NE], F32, kind="ExternalInput").ap()
        self.bias32 = nc.dram_tensor("bias32", [32, 128], F32, kind="ExternalInput").ap()
        self.ics8 = nc.dram_tensor("ics8", [NE, D], F32, kind="ExternalInput").ap()
        self.ocs8 = nc.dram_tensor("ocs8", [NE, D], F32, kind="ExternalInput").ap()
        self.y_fm = nc.dram_tensor("y_fm", [D, TOK], F32, kind="ExternalOutput").ap()

        self.k_in = [nc.dram_tensor(f"k_in{i}", [2049, TOK], BF16) for i in range(2)]
        self.k_g = [nc.dram_tensor(f"k_g{i}", [4 * 2049, TOK], BF16) for i in range(2)]
        self.v_in = [nc.dram_tensor(f"v_in{i}", [16 * TOK, DK], BF16) for i in range(2)]
        self.v_g = [nc.dram_tensor(f"v_g{i}", [4 * 16 * TOK, DK], BF16) for i in range(2)]
        self.q_sp = nc.dram_tensor("q_sp", [D, TOK], BF16)
        self.xnT_dram = nc.dram_tensor("xnT_dram", [D, TOK], BF16)
        self.stdmean_dram = nc.dram_tensor("stdmean_dram", [16, 128], F32)
        self.x2_dram = nc.dram_tensor("x2_dram", [D, TOK], BF16)
        self.ident128 = nc.inline_tensor(np.eye(128, dtype=np.float32), "ident128")
        self.ones128_bf = nc.inline_tensor(np.ones((128, 128), dtype=ml_dtypes.bfloat16), "ones128bf")
        self.ones128_f = nc.inline_tensor(np.ones((128, 128), dtype=np.float32), "ones128f")
        self.dbg = {}
        for name in self.debug_dumps:
            shape, dt = _DBG_SPECS[name]
            self.dbg[name] = nc.dram_tensor("dbg_" + name, shape, dt, kind="ExternalOutput").ap()

    def T(self, f):
        self.blk.tensor(f)

    def V(self, f):
        self.blk.vector(f)

    def A(self, f):
        self.blk.scalar(f)

    def S(self, f):
        self.blk.sync(f)

    def G(self, f):
        self.blk.gpsimd(f)

    def _dump(self, name, src):
        if name not in self.dbg:
            return
        dst = self.dbg[name]
        qg, qs, qa, pe, act, dve = self.qg, self.qs, self.qa, self.pe, self.act, self.dve

        cc = self.cc

        def f(gp, dst=dst, src=src):
            gp.wait_ge(qs.h, qs.n)
            gp.wait_ge(qa.h, qa.n)
            gp.wait_ge(cc.h, cc.n)
            gp.wait_ge(pe.h, pe.n)
            gp.wait_ge(act.h, act.n)
            gp.wait_ge(dve.h, dve.n)
            qg.inc(gp.dma_start(out=dst, in_=src), 16)
        self.G(f)

    def build(self):
        nc = self.nc
        with (
            nc.sbuf_tensor("big1", [128, DC * TOK], BF16) as big1,
            nc.sbuf_tensor("big2", [128, DC * TOK], BF16) as big2,
            nc.sbuf_tensor("wbuf0", [128, D], BF16) as wbuf0,
            nc.sbuf_tensor("wbuf1", [128, D], BF16) as wbuf1,
            nc.sbuf_tensor("stage0", [128, TOK], BF16) as stage0,
            nc.sbuf_tensor("stage1", [128, TOK], BF16) as stage1,
            nc.sbuf_tensor("ones_bf", [128, 128], BF16) as ones_bf,
            nc.sbuf_tensor("ones_f", [128, 128], F32) as ones_f,
            nc.sbuf_tensor("ident_f", [128, 128], F32) as ident_f,
            nc.sbuf_tensor("smalls", [128, 64], F32) as smalls,
            nc.sbuf_tensor("stdmean", [128, 16], F32) as stdmean,
            nc.sbuf_tensor("norms", [128, 64], F32) as norms,
            nc.semaphore("qs") as _qs,
            nc.semaphore("qa") as _qa,
            nc.semaphore("qg") as _qg,
            nc.semaphore("cc") as _cc,
            nc.semaphore("pe") as _pe,
            nc.semaphore("act") as _act,
            nc.semaphore("dve") as _dve,
            nc.Block() as blk,
        ):
            self.blk = blk
            self.big1, self.big2 = big1, big2
            self.wbuf = [wbuf0, wbuf1]
            self.stage = [stage0, stage1]
            self.ones_bf_sb, self.ones_f_sb, self.ident_f_sb = ones_bf, ones_f, ident_f
            self.smalls = smalls
            self.stdmean_sb = stdmean
            self.norms_sb = norms
            self.qs, self.qa, self.qg, self.cc = Sem(_qs), Sem(_qa), Sem(_qg), Sem(_cc)
            self.pe, self.act, self.dve = Sem(_pe), Sem(_act), Sem(_dve)

            self.p0_consts()
            self.p1_ln1()
            if self.phases >= 2:
                self.p2_proj()
            if self.phases >= 3:
                self.p3_attn()
            if self.phases >= 4:
                self.p4_wo_ln2()
            if self.phases >= 5:
                self.p5_ffn()
            self.finalize()
        return nc

    def p0_consts(self):
        qs, dve = self.qs, self.dve
        sm = self.smalls
        self.C_EPS, self.C_NEGHL = 62, 63

        def f(s):
            qs.inc(s.dma_start(out=self.ones_bf_sb[:], in_=self.ones128_bf.ap()), 16)
            qs.inc(s.dma_start(out=self.ones_f_sb[:], in_=self.ones128_f.ap()), 16)
            qs.inc(s.dma_start(out=self.ident_f_sb[:], in_=self.ident128.ap()), 16)
        self.S(f)

        def cv(v):
            v.memset(sm[:, self.C_EPS:self.C_EPS + 1], EPS)
            dve.inc(v.memset(sm[:, self.C_NEGHL:self.C_NEGHL + 1], -0.5 * float(np.log(128.0))))
        self.V(cv)

    def p1_ln1(self):
        nc, qs, act, dve = self.nc, self.qs, self.act, self.dve
        sm = self.smalls
        M_SUM, M_MEAN, M_MSQS, M_MSQ, M_TMP, M_VAR, M_LN, M_RSTD, M_NEGMR = range(9)
        with contextlib.ExitStack() as _st:
            xt0 = _st.enter_context(nc.sbuf_tensor("xt0", [128, D], F32))
            xt1 = _st.enter_context(nc.sbuf_tensor("xt1", [128, D], F32))
            xn_ts0 = _st.enter_context(nc.sbuf_tensor("xn_ts0", [128, D], BF16))
            sqscr = _st.enter_context(nc.sbuf_tensor("sqscr", [128, D], BF16))
            xt = [xt0, xt1]
            xn_ts = [xn_ts0, xn_ts0]
            x_loaded = {}
            tr_done = {}

            def load0(s):
                for t in range(2):
                    qs.inc(s.dma_start(out=xt[t][:], in_=self.x_t[t * 128:(t + 1) * 128, :]), 16)
                    x_loaded[t] = qs.n
            self.S(load0)

            for ts in range(TS):
                b = ts % 2

                def stats_v(v, ts=ts, b=b):
                    v.wait_ge(qs.h, x_loaded[ts])
                    dve.inc(v.reduce_sum(sm[:, M_SUM:M_SUM + 1], xt[b][:], axis=AX.X))
                    v.wait_ge(dve.h, dve.n)
                    dve.inc(v.tensor_scalar(sm[:, M_MEAN:M_MEAN + 1], sm[:, M_SUM:M_SUM + 1],
                                            1.0 / D, None, ALU.mult))
                self.V(stats_v)

                def stats_a(sc, ts=ts, b=b):
                    sc.wait_ge(qs.h, x_loaded[ts])
                    act.inc(sc.activation(sqscr[:], xt[b][:], AF.Square,
                                          accum_out=sm[:, M_MSQS:M_MSQS + 1]))
                self.A(stats_a)
                sq_ready = act.n

                def var_v(v, sq_ready=sq_ready):
                    v.wait_ge(act.h, sq_ready)
                    dve.inc(v.tensor_scalar(sm[:, M_MSQ:M_MSQ + 1], sm[:, M_MSQS:M_MSQS + 1],
                                            1.0 / D, None, ALU.mult))
                    dve.inc(v.tensor_tensor(sm[:, M_TMP:M_TMP + 1], sm[:, M_MEAN:M_MEAN + 1],
                                            sm[:, M_MEAN:M_MEAN + 1], ALU.mult))
                    v.wait_ge(dve.h, dve.n)
                    dve.inc(v.tensor_tensor(sm[:, M_VAR:M_VAR + 1], sm[:, M_MSQ:M_MSQ + 1],
                                            sm[:, M_TMP:M_TMP + 1], ALU.subtract))
                self.V(var_v)
                var_ready = dve.n

                def rstd_a(sc, ts=ts, var_ready=var_ready):
                    sc.wait_ge(dve.h, var_ready)
                    act.inc(sc.activation(sm[:, M_LN:M_LN + 1], sm[:, M_VAR:M_VAR + 1], AF.Ln,
                                          bias=sm[:, self.C_EPS:self.C_EPS + 1]))
                    sc.wait_ge(act.h, act.n)
                    sc.activation(self.stdmean_sb[:, 8 + ts:9 + ts], sm[:, M_LN:M_LN + 1],
                                  AF.Exp, scale=0.5)
                    act.inc(sc.activation(sm[:, M_RSTD:M_RSTD + 1], sm[:, M_LN:M_LN + 1],
                                          AF.Exp, scale=-0.5))
                self.A(rstd_a)
                rstd_ready = act.n

                def norm_v(v, ts=ts, b=b, rstd_ready=rstd_ready):
                    v.wait_ge(act.h, rstd_ready)
                    if ts >= 1:
                        v.wait_ge(qs.h, tr_done[ts - 1])
                    dve.inc(v.scalar_tensor_tensor(sm[:, M_NEGMR:M_NEGMR + 1],
                                                   sm[:, M_MEAN:M_MEAN + 1],
                                                   -1.0, sm[:, M_RSTD:M_RSTD + 1],
                                                   ALU.mult, ALU.mult))
                    v.tensor_copy(self.stdmean_sb[:, ts:ts + 1], sm[:, M_MEAN:M_MEAN + 1])
                    v.wait_ge(dve.h, dve.n)
                    dve.inc(v.tensor_scalar(xn_ts[b][:], xt[b][:], sm[:, M_RSTD:M_RSTD + 1],
                                            sm[:, M_NEGMR:M_NEGMR + 1], ALU.mult, ALU.add))
                self.V(norm_v)
                ln_done = dve.n

                def store_s(s, ts=ts, b=b, ln_done=ln_done):
                    s.wait_ge(dve.h, ln_done)
                    dst = self.big1[:].rearrange("p (dc t) -> p dc t", t=TOK)[:, :, ts * 128:(ts + 1) * 128]
                    qs.inc(s.dma_start(out=dst,
                                       in_=xn_ts[b][:].rearrange("p (dc k) -> p dc k", k=128),
                                       transpose=True), 16)
                    if ts + 2 < TS:
                        qs.inc(s.dma_start(out=xt[b][:],
                                           in_=self.x_t[(ts + 2) * 128:(ts + 3) * 128, :]), 16)
                self.S(store_s)
                tr_done[ts] = qs.n - (16 if ts + 2 < TS else 0)
                if ts + 2 < TS:
                    x_loaded[ts + 2] = qs.n

            def spill(s):
                s.wait_ge(dve.h, dve.n)
                s.wait_ge(qs.h, tr_done[TS - 1])
                qs.inc(s.dma_start(
                    out=self.xnT_dram.ap().rearrange("(dc dp) t -> dp dc t", dp=128),
                    in_=self.big1[:].rearrange("p (dc t) -> p dc t", t=TOK)), 16)
            self.S(spill)
            self.xnT_spilled = qs.n

            pe = self.pe
            sm_ps = _st.enter_context(nc.psum_tensor("sm_ps", [16, 128], F32))
            sm_tr = _st.enter_context(nc.sbuf_tensor("sm_tr", [16, 128], F32))

            def smtr(t):
                t.wait_ge(dve.h, dve.n)
                t.wait_ge(act.h, act.n)
                t.wait_ge(qs.h, qs.n)
                pe.inc(t.transpose(sm_ps[0:16, 0:128], self.stdmean_sb[:], self.ident_f_sb[:]))
            self.T(smtr)
            smtr_done = pe.n

            def smev(sc, smtr_done=smtr_done):
                sc.wait_ge(pe.h, smtr_done)
                act.inc(sc.activation(sm_tr[:], sm_ps[0:16, 0:128], AF.Copy))
            self.A(smev)
            smev_done = act.n

            def smst(s, smev_done=smev_done):
                s.wait_ge(act.h, smev_done)
                qs.inc(s.dma_start(out=self.stdmean_dram.ap(), in_=sm_tr[:]), 16)
            self.S(smst)
            self.stdmean_stored = qs.n
            self.p1_act_end = act.n

            def fence(t):
                t.wait_ge(qs.h, qs.n)
            self.T(fence)
        self.xnT_done = self.xnT_spilled
        self._dump("xnT", self.big1[:])

    def p2_proj(self):
        nc = self.nc
        qs, qa, qg, cc, pe, act = self.qs, self.qa, self.qg, self.cc, self.pe, self.act
        dve = self.dve
        with contextlib.ExitStack() as _st:
            proj_ps0 = _st.enter_context(nc.psum_tensor("proj_ps0", [128, TOK], F32))
            proj_ps1 = _st.enter_context(nc.psum_tensor("proj_ps1", [128, TOK], F32))
            norm_ps = _st.enter_context(nc.psum_tensor("norm_ps", [128, TOK], F32))
            vt_stage = _st.enter_context(nc.sbuf_tensor("vt_stage", [128, TOK], BF16))
            sqst0 = _st.enter_context(nc.sbuf_tensor("sqst0", [128, TOK], BF16))
            sqst1 = _st.enter_context(nc.sbuf_tensor("sqst1", [128, TOK], BF16))
            proj_ps = [proj_ps0, proj_ps1]
            mm_done, evac_done, store_done, w_loaded = {}, {}, {}, {}
            nmm_done, nred_done = {}, {}
            self.ag_done = {}

            mats = [("k", self.w_blk["wk"]), ("v", self.w_blk["wv"]), ("q", self.w_blk["wq"])]
            for mi, (mname, wsrc) in enumerate(mats):
                gi = mi * DC
                for ec in range(DC):
                    g = gi + ec
                    wp, sp = g % 2, g % 2

                    def wload(gp, g=g, ec=ec, wp=wp, wsrc=wsrc):
                        if g >= 2:
                            gp.wait_ge(pe.h, mm_done[g - 2])
                        qg.inc(gp.dma_start(out=self.wbuf[wp][:],
                                            in_=wsrc[ec].rearrange("p dc e -> p (dc e)")), 16)
                    self.G(wload)
                    w_loaded[g] = qg.n

                    def mm(t, g=g, wp=wp, sp=sp):
                        t.wait_ge(qg.h, w_loaded[g])
                        if g == 0:
                            t.wait_ge(qs.h, self.xnT_done)
                            t.wait_ge(act.h, self.p1_act_end)
                        if g >= 2:
                            t.wait_ge(act.h, evac_done[g - 2])
                        ins = None
                        for dc in range(DC):
                            for hf in range(2):
                                ins = t.matmul(proj_ps[sp][:, hf * 512:(hf + 1) * 512],
                                               self.wbuf[wp][:, dc * 128:(dc + 1) * 128],
                                               self.big1[:, dc * TOK + hf * 512:dc * TOK + (hf + 1) * 512],
                                               start=(dc == 0), stop=(dc == DC - 1))
                        pe.inc(ins)
                    self.T(mm)
                    mm_done[g] = pe.n

                    def evac(sc, g=g, sp=sp, mname=mname):
                        sc.wait_ge(pe.h, mm_done[g])
                        if g >= 2:
                            sc.wait_ge(qs.h, store_done[g - 2])
                        act.inc(sc.activation(self.stage[sp][:], proj_ps[sp][:], AF.Copy))
                        if mname in ("k", "q"):
                            if g >= 2 and (g - 2) in nmm_done:
                                sc.wait_ge(pe.h, nmm_done[g - 2])
                            act.inc(sc.activation([sqst0, sqst1][sp][:], self.stage[sp][:],
                                                  AF.Square))
                    self.A(evac)
                    evac_done[g] = act.n

                    if mname in ("k", "q"):
                        def nmm(t, g=g, ec=ec, sp=sp):
                            t.wait_ge(act.h, evac_done[g])
                            if g >= 2 and (g - 2) in nred_done:
                                t.wait_ge(dve.h, nred_done[g - 2])
                            ins = None
                            for hf in range(2):
                                ins = t.matmul(norm_ps[:, hf * 512:(hf + 1) * 512],
                                               self.ones_bf_sb[:],
                                               [sqst0, sqst1][sp][:, hf * 512:(hf + 1) * 512],
                                               start=True, stop=True)
                            pe.inc(ins)
                        self.T(nmm)
                        nmm_done[g] = pe.n

                        def nred(v, g=g, ec=ec, mname=mname):
                            v.wait_ge(pe.h, nmm_done[g])
                            col = ec if mname == "q" else 32 + ec
                            dve.inc(v.reduce_max(self.norms_sb[:, col:col + 1],
                                                 norm_ps[:], axis=AX.X))
                        self.V(nred)
                        nred_done[g] = dve.n

                    if mname == "k":
                        def store(s, g=g, ec=ec, sp=sp):
                            s.wait_ge(act.h, evac_done[g])
                            half, row = (0, ec * 128) if ec < 16 else (1, (ec - 16) * 128)
                            qs.inc(s.dma_start(out=self.k_in[half].ap()[row:row + 128, :],
                                               in_=self.stage[sp][:]), 16)
                        self.S(store)
                        store_done[g] = qs.n
                    elif mname == "v":
                        def vtr(sc, g=g, ec=ec, sp=sp):
                            sc.wait_ge(act.h, evac_done[g])
                            if ec >= 1:
                                sc.wait_ge(qs.h, store_done[g - 1])
                            qa.inc(sc.dma_start(
                                out=vt_stage[:].rearrange("p (ts v) -> p ts v", v=128),
                                in_=self.stage[sp][:].rearrange("p (ts k) -> p ts k", k=128),
                                transpose=True), 16)
                        self.A(vtr)
                        vtr_n = qa.n

                        def store(s, g=g, ec=ec, vtr_n=vtr_n):
                            s.wait_ge(qa.h, vtr_n)
                            half, row = (0, ec * TOK) if ec < 16 else (1, (ec - 16) * TOK)
                            qs.inc(s.dma_start(
                                out=self.v_in[half].ap()[row:row + TOK, :]
                                    .rearrange("(p ts) v -> p ts v", ts=8),
                                in_=vt_stage[:].rearrange("p (ts v) -> p ts v", v=128)), 16)
                        self.S(store)
                        store_done[g] = qs.n
                    else:
                        def store(s, g=g, ec=ec, sp=sp):
                            s.wait_ge(act.h, evac_done[g])
                            qs.inc(s.dma_start(out=self.q_sp.ap()[ec * 128:(ec + 1) * 128, :],
                                               in_=self.stage[sp][:]), 16)
                        self.S(store)
                        store_done[g] = qs.n


            self.q_store_done = store_done[3 * DC - 1]

            def kmaxrow(g):
                g.wait_ge(dve.h, nred_done[31])
                qg.inc(g.dma_start(out=self.k_in[0].ap()[2048:2049, 0:16],
                                   in_=self.norms_sb[0:1, 32:48]), 16)
                qg.inc(g.dma_start(out=self.k_in[1].ap()[2048:2049, 0:16],
                                   in_=self.norms_sb[0:1, 48:64]), 16)
            self.G(kmaxrow)

            def ags(gp):
                gp.wait_ge(qs.h, qs.n)
                gp.wait_ge(qa.h, qa.n)
                gp.wait_ge(qg.h, qg.n)
                for mname, src_t, dst_t in (("k", self.k_in, self.k_g),
                                            ("v", self.v_in, self.v_g)):
                    for half in range(2):
                        cc.inc(gp.collective_compute(
                            "AllGather", ALU.bypass,
                            replica_groups=[[0, 1, 2, 3], [4, 5, 6, 7]],
                            ins=[src_t[half].ap().opt()],
                            outs=[dst_t[half].ap().opt()]))
                        self.ag_done[(mname, half)] = cc.n
            self.G(ags)

            def fence(t):
                t.wait_ge(qs.h, qs.n)
                t.wait_ge(qa.h, qa.n)
            self.T(fence)
        self._dump("k_own", self.k_in[0].ap())
        self._dump("q_own", self.q_sp.ap())
        self._dump("kg0", self.k_g[0].ap())
        self._dump("vg0", self.v_g[0].ap())

    def p3_attn(self):
        nc = self.nc
        qs, qg, cc, pe, act, dve = self.qs, self.qg, self.cc, self.pe, self.act, self.dve
        sm = self.smalls
        C_LP, C_LSUM, C_RL = 44, 60, 61
        with contextlib.ExitStack() as _st:
            k_h0 = _st.enter_context(nc.sbuf_tensor("k_h0", [128, 4096], BF16))
            k_h1 = _st.enter_context(nc.sbuf_tensor("k_h1", [128, 4096], BF16))
            v_h = _st.enter_context(nc.sbuf_tensor("v_h", [128, 4096], BF16))
            q_h0 = _st.enter_context(nc.sbuf_tensor("q_h0", [128, TOK], BF16))
            q_h1 = _st.enter_context(nc.sbuf_tensor("q_h1", [128, TOK], BF16))
            km4 = _st.enter_context(nc.sbuf_tensor("km4", [1, 128], F32))
            cneg_all = _st.enter_context(nc.sbuf_tensor("cneg_all", [128, 32], F32))
            at_tm0 = _st.enter_context(nc.sbuf_tensor("at_tm0", [128, 4096], BF16))
            at_tm1 = _st.enter_context(nc.sbuf_tensor("at_tm1", [128, 4096], BF16))
            at_tm2, at_tm3 = self.wbuf
            s_ps0 = _st.enter_context(nc.psum_tensor("s_ps0", [128, TOK], F32))
            s_ps1 = _st.enter_context(nc.psum_tensor("s_ps1", [128, TOK], F32))
            o_ps = _st.enter_context(nc.psum_tensor("o_ps", [128, TOK], F32))
            c_ps = _st.enter_context(nc.psum_tensor("c_ps", [128, 512], F32))
            at_tm = [at_tm0, at_tm1, at_tm2, at_tm3]
            s_ps = [s_ps0, s_ps1]
            k_hb = [k_h0, k_h1]
            q_hb = [q_h0, q_h1]
            kq_loaded = {}
            v_loaded = {}
            s_done = {}
            exp_done = {}
            exp_qt_done = {}
            scale_done = {}
            tr_done = {}
            tr_done7b = {}
            av_done = {}
            av_pred = {}
            oevac_done = {}

            def emit_kq_load(g, h):
                half, hh = (0, h) if h < 16 else (1, h - 16)
                kg = self.k_g[half].ap()
                kb = k_hb[h % 2]
                for r in range(4):
                    qg.inc(g.dma_start(
                        out=kb[:, r * 1024:(r + 1) * 1024],
                        in_=kg[r * 2049 + hh * 128: r * 2049 + hh * 128 + 128, :]), 16)
                qg.inc(g.dma_start(out=q_hb[h % 2][:],
                                   in_=self.q_sp.ap()[h * 128:(h + 1) * 128, :]), 16)

            def emit_v_load(g, h):
                half, hh = (0, h) if h < 16 else (1, h - 16)
                vg = self.v_g[half].ap()
                for r in range(4):
                    src = vg[r * 16 * TOK + hh * TOK: r * 16 * TOK + (hh + 1) * TOK, :]
                    qg.inc(g.dma_start(
                        out=v_h[:, r * 1024:(r + 1) * 1024].rearrange("p (ts v) -> p ts v", v=128),
                        in_=src.rearrange("(p ts) v -> p ts v", ts=8)), 16)

            def prol_g(g):
                g.wait_ge(qs.h, self.q_store_done)
                g.wait_ge(cc.h, self.ag_done[("v", 1)])
                for half in range(2):
                    kg = self.k_g[half].ap()
                    for r in range(4):
                        qg.inc(g.dma_start(
                            out=km4[0:1, half * 64 + r * 16: half * 64 + r * 16 + 16],
                            in_=kg[r * 2049 + 2048: r * 2049 + 2049, 0:16]), 16)
                emit_kq_load(g, 0)
                emit_v_load(g, 0)
                emit_kq_load(g, 1)
            self.G(prol_g)
            km_loaded = qg.n - 9 * 16 - 4 * 16
            kq_loaded[0] = qg.n - 9 * 16
            v_loaded[0] = qg.n - 5 * 16
            kq_loaded[1] = qg.n

            def prol_v(v):
                v.wait_ge(qg.h, km_loaded)
                for half in range(2):
                    seg = km4[0:1, half * 64:(half + 1) * 64]
                    dve.inc(v.reduce_max(
                        sm[0:1, half * 16:(half + 1) * 16],
                        seg.rearrange("p (r hh) -> p hh r", hh=16), axis=AX.X))
                v.wait_ge(dve.h, dve.n)
                dve.inc(v.tensor_tensor(sm[0:1, 0:32], sm[0:1, 0:32],
                                        self.norms_sb[0:1, 0:32], ALU.mult))
            self.V(prol_v)
            prod_done = dve.n

            def prol_a(sc, prod_done=prod_done):
                sc.wait_ge(dve.h, prod_done)
                act.inc(sc.activation(sm[0:1, 0:32], sm[0:1, 0:32], AF.Ln))
                sc.wait_ge(act.h, act.n)
                act.inc(sc.activation(sm[0:1, 0:32], sm[0:1, 0:32],
                                      AF.Exp, scale=0.5,
                                      bias=sm[0:1, self.C_NEGHL:self.C_NEGHL + 1]))
            self.A(prol_a)
            c_act = act.n

            def prol_v2(v, c_act=c_act):
                v.wait_ge(act.h, c_act)
                dve.inc(v.tensor_scalar(sm[0:1, 0:32], sm[0:1, 0:32],
                                        -1.0, None, ALU.mult))
            self.V(prol_v2)
            cneg_row = dve.n

            def prol_t(t, cneg_row=cneg_row):
                t.wait_ge(dve.h, cneg_row)
                pe.inc(t.matmul(c_ps[0:128, 0:32], self.ones_f_sb[0:1, :],
                                sm[0:1, 0:32], start=True, stop=True))
            self.T(prol_t)
            cbc = pe.n

            def prol_a2(sc, cbc=cbc):
                sc.wait_ge(pe.h, cbc)
                act.inc(sc.activation(cneg_all[:], c_ps[0:128, 0:32], AF.Copy))
            self.A(prol_a2)
            cneg_done = act.n

            for h in range(NH):
                act0 = act.n
                for qt in range(TS):
                    for kc in range(4):
                        exp_done[h * 32 + qt * 4 + kc] = act0 + qt * 4 + kc + 1
                    exp_qt_done[(h, qt)] = act0 + qt * 4 + 4
                qs0, qa0, dve0 = qs.n, self.qa.n, dve.n
                for qt in range(0, TS, 2):
                    tr_done[h * TS + qt] = (qs.h, qs0 + (qt // 2 + 1) * 16)
                for qt in range(1, TS, 2):
                    tr_done[h * TS + qt] = (self.qa.h, qa0 + ((qt + 1) // 2) * 16)

                for qt in range(TS):
                    gq = h * TS + qt
                    for kc in range(4):
                        gsc = gq * 4 + kc
                        sp = gsc % 2

                        def smm(t, h=h, qt=qt, kc=kc, sp=sp, gsc=gsc):
                            if gsc == 0:
                                t.wait_ge(qg.h, kq_loaded[0])
                                t.wait_ge(act.h, cneg_done)
                            if gsc >= 2:
                                t.wait_ge(act.h, exp_done[gsc - 2])
                            if kc == 0 and qt == 0:
                                t.wait_ge(qg.h, kq_loaded[h])
                            ins = None
                            for hf in range(2):
                                col = kc * 1024 + hf * 512
                                ins = t.matmul(s_ps[sp][:, hf * 512:(hf + 1) * 512],
                                               q_hb[h % 2][:, qt * 128:(qt + 1) * 128],
                                               k_hb[h % 2][:, col:col + 512],
                                               start=True, stop=True)
                            pe.inc(ins)
                        self.T(smm)
                        s_done[gsc] = pe.n

                def head_act(sc, h=h, dve0=dve0):
                    def issue_tra(qt):
                        sc.wait_ge(dve.h, dve0 + 3 * (qt + 1))
                        if h >= 1 and qt == 1:
                            sc.wait_ge(pe.h, av_done[h - 1])
                        gq2 = h * TS + qt
                        ktn = 16 if qt == 7 else 32
                        dstv2 = self.big1[:].rearrange("p (kt q) -> p kt q", q=TOK)[:, 0:ktn, qt * 128:(qt + 1) * 128]
                        self.qa.inc(sc.dma_start(
                            out=dstv2,
                            in_=at_tm[gq2 % 4][:, 0:ktn * 128].rearrange("p (kt k) -> p kt k", k=128),
                            transpose=True), 16)
                    for qt in range(TS):
                        gq = h * TS + qt
                        tp = gq % 4
                        for kc in range(4):
                            gsc = gq * 4 + kc
                            sc.wait_ge(pe.h, s_done[gsc])
                            if kc == 0 and gq >= 4:
                                semh, val = tr_done[gq - 4]
                                sc.wait_ge(semh, val)
                                if (gq - 4) % TS == 7:
                                    semh2, val2 = tr_done7b[(gq - 4) // TS]
                                    sc.wait_ge(semh2, val2)
                            lp = C_LP + (qt % 4) * 4 + kc
                            act.inc(sc.activation(at_tm[tp][:, kc * 1024:(kc + 1) * 1024],
                                                  s_ps[gsc % 2][:], AF.Exp, scale=INV_SQRT_DK,
                                                  bias=cneg_all[:, h:h + 1],
                                                  accum_out=sm[:, lp:lp + 1]))
                        if qt in (2, 4, 6):
                            issue_tra(qt - 1)
                    issue_tra(7)
                self.A(head_act)
                assert act.n == act0 + 32, (act.n, act0)
                assert self.qa.n == qa0 + 64, (self.qa.n, qa0)

                for qt in range(TS):
                    gq = h * TS + qt
                    tp = gq % 4

                    def scl(v, h=h, qt=qt, tp=tp):
                        v.wait_ge(act.h, exp_qt_done[(h, qt)])
                        lp0 = C_LP + (qt % 4) * 4
                        dve.inc(v.reduce_sum(sm[:, C_LSUM:C_LSUM + 1], sm[:, lp0:lp0 + 4],
                                             axis=AX.X))
                        v.wait_ge(dve.h, dve.n)
                        dve.inc(v.reciprocal(sm[:, C_RL:C_RL + 1], sm[:, C_LSUM:C_LSUM + 1]))
                        v.wait_ge(dve.h, dve.n)
                        dve.inc(v.tensor_scalar(at_tm[tp][:], at_tm[tp][:],
                                                sm[:, C_RL:C_RL + 1], None, ALU.mult))
                    self.V(scl)
                    scale_done[(h, qt)] = dve.n
                    assert scale_done[(h, qt)] == dve0 + 3 * (qt + 1)

                    if qt % 2 == 0:
                        dstv = self.big1[:].rearrange("p (kt q) -> p kt q", q=TOK)[:, :, qt * 128:(qt + 1) * 128]

                        def trs(s, h=h, qt=qt, tp=tp, dstv=dstv):
                            s.wait_ge(dve.h, scale_done[(h, qt)])
                            if h >= 1 and qt == 0:
                                s.wait_ge(pe.h, av_done[h - 1])
                            qs.inc(s.dma_start(out=dstv,
                                               in_=at_tm[tp][:].rearrange("p (kt k) -> p kt k", k=128),
                                               transpose=True), 16)
                        self.S(trs)
                        assert tr_done[gq] == (qs.h, qs.n), (tr_done[gq], qs.n)

                def trs7b(s, h=h):
                    s.wait_ge(dve.h, scale_done[(h, 7)])
                    dstv = self.big1[:].rearrange("p (kt q) -> p kt q", q=TOK)[:, 16:32, 7 * 128:8 * 128]
                    qs.inc(s.dma_start(
                        out=dstv,
                        in_=at_tm[(h * TS + 7) % 4][:, 2048:4096].rearrange("p (kt k) -> p kt k", k=128),
                        transpose=True), 16)
                self.S(trs7b)
                tr_done7b[h] = (qs.h, qs.n)

                if h + 2 < NH:
                    def pf_kq(g, h=h):
                        g.wait_ge(pe.h, s_done[(h * TS + TS - 1) * 4 + 3])
                        emit_kq_load(g, h + 2)
                    self.G(pf_kq)
                    kq_loaded[h + 2] = qg.n

                def av(t, h=h):
                    t.wait_ge(qg.h, v_loaded[h])
                    if h >= 1:
                        t.wait_ge(dve.h, oevac_done[h - 1])
                    ins = None
                    for hf in range(2):
                        for qt in range(hf * 4, hf * 4 + 4):
                            semh, val = tr_done[h * TS + qt]
                            t.wait_ge(semh, val)
                        if hf == 1:
                            semh, val = tr_done7b[h]
                            t.wait_ge(semh, val)
                        for kt in range(NH):
                            ins = t.matmul(o_ps[:, hf * 512:(hf + 1) * 512],
                                           v_h[:, kt * 128:(kt + 1) * 128],
                                           self.big1[:, kt * TOK + hf * 512:kt * TOK + (hf + 1) * 512],
                                           start=(kt == 0), stop=(kt == NH - 1))
                    pe.inc(ins)
                self.T(av)
                av_done[h] = pe.n

                def oevac(v, h=h):
                    v.wait_ge(pe.h, av_done[h])
                    dve.inc(v.tensor_copy(self.big2[:, h * TOK:(h + 1) * TOK], o_ps[:]))
                self.V(oevac)
                oevac_done[h] = dve.n

                if h + 1 < NH:
                    def pf_v(g, h=h):
                        g.wait_ge(pe.h, av_done[h])
                        emit_v_load(g, h + 1)
                    self.G(pf_v)
                    v_loaded[h + 1] = qg.n

            self.attn_done_dve = dve.n
            self.attn_done = act.n
            self.attn_qs_done = qs.n
            self.attn_qa_done = self.qa.n
            self.attn_pe_done = pe.n

            def fence(t):
                t.wait_ge(dve.h, self.attn_done_dve)
                t.wait_ge(act.h, self.attn_done)
                t.wait_ge(qs.h, qs.n)
                t.wait_ge(self.qa.h, self.qa.n)
            self.T(fence)
        self._dump("o_fm", self.big2[:])

    def p4_wo_ln2(self):
        nc = self.nc
        qs, qa, qg, pe, act, dve = self.qs, self.qa, self.qg, self.pe, self.act, self.dve
        with contextlib.ExitStack() as _st:
            y_ps0 = _st.enter_context(nc.psum_tensor("y_ps0", [128, TOK], F32))
            y_ps1 = _st.enter_context(nc.psum_tensor("y_ps1", [128, TOK], F32))
            mean_ps = _st.enter_context(nc.psum_tensor("mean_ps", [128, TOK], F32))
            msq_ps = _st.enter_context(nc.psum_tensor("msq_ps", [128, TOK], F32))
            xfm0 = _st.enter_context(nc.sbuf_tensor("xfm0", [128, TOK], BF16))
            xfm1 = _st.enter_context(nc.sbuf_tensor("xfm1", [128, TOK], BF16))
            std_rep = _st.enter_context(nc.sbuf_tensor("std_rep", [128, TOK], F32))
            mean_rep = _st.enter_context(nc.sbuf_tensor("mean_rep", [128, TOK], F32))
            x2c0 = _st.enter_context(nc.sbuf_tensor("x2c0", [128, TOK], BF16))
            x2c1 = _st.enter_context(nc.sbuf_tensor("x2c1", [128, TOK], BF16))
            sqc = _st.enter_context(nc.sbuf_tensor("sqc", [128, TOK], F32))
            rstd_rep = _st.enter_context(nc.sbuf_tensor("rstd_rep", [128, TOK], F32))
            negmr_rep = _st.enter_context(nc.sbuf_tensor("negmr_rep", [128, TOK], F32))
            tchunk = _st.enter_context(nc.sbuf_tensor("tchunk", [128, TOK], F32))
            y_ps = [y_ps0, y_ps1]
            xfm = [xfm0, xfm1]
            x2c = [x2c0, x2c1]
            w_loaded, mm_done, x2_done, sq_done = {}, {}, {}, {}
            stat_done, store_done, xfm_loaded = {}, {}, {}

            def xfm_load(sc, oc, b):
                qa.inc(sc.dma_start(out=xfm[b][:],
                                    in_=self.xnT_dram.ap()[oc * 128:(oc + 1) * 128, :]), 16)

            def first(sc):
                sc.wait_ge(act.h, self.attn_done)
                sc.wait_ge(dve.h, self.attn_done_dve)
                sc.wait_ge(qs.h, self.attn_qs_done)
                sc.wait_ge(qa.h, self.attn_qa_done)
                sc.wait_ge(pe.h, self.attn_pe_done)
                xfm_load(sc, 0, 0)
                xfm_load(sc, 1, 1)
                for ts in range(TS):
                    qa.inc(sc.dma_start(
                        out=mean_rep[:, ts * 128:(ts + 1) * 128],
                        in_=self.stdmean_dram.ap()[ts:ts + 1, :].partition_broadcast(128)), 16)
                    qa.inc(sc.dma_start(
                        out=std_rep[:, ts * 128:(ts + 1) * 128],
                        in_=self.stdmean_dram.ap()[8 + ts:9 + ts, :].partition_broadcast(128)), 16)
            self.A(first)
            xfm_loaded[0] = qa.n - 16 * 17
            xfm_loaded[1] = qa.n - 16 * 16
            self.rep_loaded = qa.n

            for oc in range(DC):
                wp, sp = oc % 2, oc % 2

                def wload(gp, oc=oc, wp=wp):
                    if oc == 0:
                        gp.wait_ge(act.h, self.attn_done)
                        gp.wait_ge(dve.h, self.attn_done_dve)
                        gp.wait_ge(qs.h, self.attn_qs_done)
                        gp.wait_ge(qa.h, self.attn_qa_done)
                        gp.wait_ge(pe.h, self.attn_pe_done)
                    if oc >= 2:
                        gp.wait_ge(pe.h, mm_done[oc - 2])
                    qg.inc(gp.dma_start(out=self.wbuf[wp][:],
                                        in_=self.w_blk["wo"][oc].rearrange("p dc e -> p (dc e)")), 16)
                self.G(wload)
                w_loaded[oc] = qg.n

                def mm(t, oc=oc, wp=wp, sp=sp):
                    t.wait_ge(qg.h, w_loaded[oc])
                    if oc >= 2:
                        t.wait_ge(dve.h, x2_done[oc - 2])
                    ins = None
                    for ec in range(DC):
                        for hf in range(2):
                            ins = t.matmul(y_ps[sp][:, hf * 512:(hf + 1) * 512],
                                           self.wbuf[wp][:, ec * 128:(ec + 1) * 128],
                                           self.big2[:, ec * TOK + hf * 512:ec * TOK + (hf + 1) * 512],
                                           start=(ec == 0), stop=(ec == DC - 1))
                    pe.inc(ins)
                self.T(mm)
                mm_done[oc] = pe.n

                def addres(v, oc=oc, sp=sp):
                    v.wait_ge(pe.h, mm_done[oc])
                    v.wait_ge(qa.h, xfm_loaded[oc])
                    if oc == 0:
                        v.wait_ge(qa.h, self.rep_loaded)
                    if oc >= 2:
                        v.wait_ge(qs.h, store_done[oc - 2])
                        v.wait_ge(pe.h, stat_done[oc - 2])
                    v.tensor_tensor(tchunk[:], xfm[sp][:], std_rep[:], ALU.mult)
                    v.tensor_tensor(tchunk[:], tchunk[:], mean_rep[:], ALU.add)
                    dve.inc(v.tensor_tensor(x2c[sp][:], y_ps[sp][:], tchunk[:], ALU.add))
                self.V(addres)
                x2_done[oc] = dve.n

                def sq(sc, oc=oc, sp=sp):
                    sc.wait_ge(dve.h, x2_done[oc])
                    act.inc(sc.activation(sqc[:], x2c[sp][:], AF.Square))
                    if oc + 2 < DC:
                        xfm_load(sc, oc + 2, sp)
                self.A(sq)
                sq_done[oc] = act.n
                if oc + 2 < DC:
                    xfm_loaded[oc + 2] = qa.n

                def stats(t, oc=oc, sp=sp):
                    t.wait_ge(act.h, sq_done[oc])
                    for hf in range(2):
                        t.matmul(mean_ps[:, hf * 512:(hf + 1) * 512], self.ones_bf_sb[:],
                                 x2c[sp][:, hf * 512:(hf + 1) * 512],
                                 start=(oc == 0), stop=(oc == DC - 1))
                    ins = None
                    for hf in range(2):
                        ins = t.matmul(msq_ps[:, hf * 512:(hf + 1) * 512], self.ones_f_sb[:],
                                       sqc[:, hf * 512:(hf + 1) * 512],
                                       start=(oc == 0), stop=(oc == DC - 1))
                    pe.inc(ins)
                self.T(stats)
                stat_done[oc] = pe.n

                def store(s, oc=oc, sp=sp):
                    s.wait_ge(dve.h, x2_done[oc])
                    qs.inc(s.dma_start(out=self.x2_dram.ap()[oc * 128:(oc + 1) * 128, :],
                                       in_=x2c[sp][:]), 16)
                self.S(store)
                store_done[oc] = qs.n

            def ln2_v(v):
                v.wait_ge(pe.h, stat_done[DC - 1])
                v.tensor_scalar(rstd_rep[:], msq_ps[:], 1.0 / D, None, ALU.mult)
                v.tensor_scalar(negmr_rep[:], mean_ps[:], 1.0 / D, None, ALU.mult)
                v.tensor_tensor(tchunk[:], negmr_rep[:], negmr_rep[:], ALU.mult)
                dve.inc(v.tensor_tensor(rstd_rep[:], rstd_rep[:], tchunk[:], ALU.subtract))
            self.V(ln2_v)
            var_done = dve.n

            def ln2_a(sc, var_done=var_done):
                sc.wait_ge(dve.h, var_done)
                act.inc(sc.activation(tchunk[:], rstd_rep[:], AF.Ln,
                                      bias=self.smalls[:, self.C_EPS:self.C_EPS + 1]))
                sc.wait_ge(act.h, act.n)
                act.inc(sc.activation(rstd_rep[:], tchunk[:], AF.Exp, scale=-0.5))
            self.A(ln2_a)
            rstd_done = act.n

            def ln2_v2(v, rstd_done=rstd_done):
                v.wait_ge(act.h, rstd_done)
                v.tensor_scalar(tchunk[:], negmr_rep[:], -1.0, None, ALU.mult)
                dve.inc(v.tensor_tensor(negmr_rep[:], tchunk[:], rstd_rep[:], ALU.mult))
            self.V(ln2_v2)
            nmr_done = dve.n

            x2_loaded = {}

            def x2l0(s, nmr_done=nmr_done):
                s.wait_ge(dve.h, nmr_done)
                for b in range(2):
                    qs.inc(s.dma_start(out=x2c[b][:],
                                       in_=self.x2_dram.ap()[b * 128:(b + 1) * 128, :]), 16)
                    x2_loaded[b] = qs.n
            self.S(x2l0)

            norm_done = {}
            for dc in range(DC):
                b = dc % 2

                def normc(v, dc=dc, b=b):
                    v.wait_ge(qs.h, x2_loaded[dc])
                    v.tensor_tensor(tchunk[:], x2c[b][:], rstd_rep[:], ALU.mult)
                    dve.inc(v.tensor_tensor(self.big2[:, dc * TOK:(dc + 1) * TOK],
                                            tchunk[:], negmr_rep[:], ALU.add))
                self.V(normc)
                norm_done[dc] = dve.n

                if dc + 2 < DC:
                    def x2l(s, dc=dc, b=b):
                        s.wait_ge(dve.h, norm_done[dc])
                        qs.inc(s.dma_start(out=x2c[b][:],
                                           in_=self.x2_dram.ap()[(dc + 2) * 128:(dc + 3) * 128, :]), 16)
                    self.S(x2l)
                    x2_loaded[dc + 2] = qs.n

            self.xn2T_done = dve.n

            def fence(t):
                t.wait_ge(dve.h, self.xn2T_done)
                t.wait_ge(qs.h, qs.n)
                t.wait_ge(qa.h, qa.n)
            self.T(fence)
        self._dump("x2", self.x2_dram.ap())
        self._dump("xn2T", self.big2[:])

    def p5_ffn(self):
        nc = self.nc
        qs, qa, qg, pe, act, dve = self.qs, self.qa, self.qg, self.pe, self.act, self.dve
        with contextlib.ExitStack() as _st:
            rt_ps = _st.enter_context(nc.psum_tensor("rt_ps", [8, TOK], F32))
            aux_ps = _st.enter_context(nc.psum_tensor("aux_ps", [128, TOK], F32))
            f_ps0 = _st.enter_context(nc.psum_tensor("f_ps0", [128, TOK], F32))
            f_ps1 = _st.enter_context(nc.psum_tensor("f_ps1", [128, TOK], F32))
            gwt_sb = _st.enter_context(nc.sbuf_tensor("gwt_sb", [128, DC * NE], BF16))
            ics_sb = _st.enter_context(nc.sbuf_tensor("ics_sb", [8, D], BF16))
            ocs_sb = _st.enter_context(nc.sbuf_tensor("ocs_sb", [8, D], BF16))
            et_sb = _st.enter_context(nc.sbuf_tensor("et_sb", [8, TOK], F32))
            rs_rep = _st.enter_context(nc.sbuf_tensor("rs_rep", [128, TOK], F32))
            r_sb = _st.enter_context(nc.sbuf_tensor("r_sb", [8, TOK], BF16))
            bias_cols = _st.enter_context(nc.sbuf_tensor("bias_cols", [128, DC], F32))
            bias32_sb = _st.enter_context(nc.sbuf_tensor("bias32_sb", [32, 128], F32))
            wf32 = _st.enter_context(nc.sbuf_tensor("wf32", [128, D], F32))
            swf0, swf1 = self.wbuf
            osc_sb0, osc_sb1 = self.stage
            x2c0b = _st.enter_context(nc.sbuf_tensor("x2c0b", [128, TOK], BF16))
            x2c1b = _st.enter_context(nc.sbuf_tensor("x2c1b", [128, TOK], BF16))
            tch = _st.enter_context(nc.sbuf_tensor("tch", [128, TOK], F32))
            out0 = _st.enter_context(nc.sbuf_tensor("out0", [128, TOK], F32))
            out1 = out0
            swf = [swf0, swf1]
            osc_sb = [osc_sb0, osc_sb1]
            x2c = [x2c0b, x2c1b]
            outb = [out0, out1]
            f_ps = [f_ps0, f_ps1]

            def sl(gp):
                gp.wait_ge(dve.h, self.xn2T_done)
                qg.inc(gp.dma_start(out=gwt_sb[:], in_=self.gwt_h), 16)
                qg.inc(gp.dma_start(out=ics_sb[:], in_=self.ics8), 16)
                qg.inc(gp.dma_start(out=ocs_sb[:], in_=self.ocs8), 16)
            self.G(sl)
            small_loaded = qg.n

            def bl(s):
                s.wait_ge(dve.h, self.xn2T_done)
                qs.inc(s.dma_start(out=bias32_sb[:], in_=self.bias32), 16)
            self.S(bl)
            bias_loaded = qs.n

            def btr(t):
                t.wait_ge(qs.h, bias_loaded)
                t.wait_ge(dve.h, self.xn2T_done)
                pe.inc(t.transpose(aux_ps[0:128, 0:32], bias32_sb[:],
                                   self.ident_f_sb[0:32, 0:32]))
            self.T(btr)
            btr_done = pe.n

            def bev(sc, btr_done=btr_done):
                sc.wait_ge(pe.h, btr_done)
                act.inc(sc.activation(bias_cols[:], aux_ps[0:128, 0:32], AF.Copy))
            self.A(bev)
            bias_done = act.n

            def rmm(t):
                t.wait_ge(qg.h, small_loaded)
                t.wait_ge(act.h, bias_done)
                ins = None
                for dc in range(DC):
                    for hf in range(2):
                        ins = t.matmul(rt_ps[:, hf * 512:(hf + 1) * 512],
                                       gwt_sb[:, dc * NE:(dc + 1) * NE],
                                       self.big2[:, dc * TOK + hf * 512:dc * TOK + (hf + 1) * 512],
                                       start=(dc == 0), stop=(dc == DC - 1))
                pe.inc(ins)
            self.T(rmm)
            rmm_done = pe.n

            def rexp(sc, rmm_done=rmm_done):
                sc.wait_ge(pe.h, rmm_done)
                act.inc(sc.activation(et_sb[:], rt_ps[:], AF.Exp))
            self.A(rexp)
            rexp_done = act.n

            def rsum(t, rexp_done=rexp_done):
                t.wait_ge(act.h, rexp_done)
                ins = None
                for hf in range(2):
                    ins = t.matmul(aux_ps[:, hf * 512:(hf + 1) * 512], self.ones_f_sb[0:8, :],
                                   et_sb[:, hf * 512:(hf + 1) * 512], start=True, stop=True)
                pe.inc(ins)
            self.T(rsum)
            rsum_done = pe.n

            def rrec(sc, rsum_done=rsum_done):
                sc.wait_ge(pe.h, rsum_done)
                act.inc(sc.activation(rs_rep[:], aux_ps[:], AF.Ln))
                sc.wait_ge(act.h, act.n)
                act.inc(sc.activation(rs_rep[:], rs_rep[:], AF.Exp, scale=-1.0))
            self.A(rrec)
            rrec_done = act.n

            def rnorm(v, rrec_done=rrec_done):
                v.wait_ge(act.h, rrec_done)
                dve.inc(v.tensor_tensor(r_sb[:], et_sb[:], rs_rep[0:8, :], ALU.mult))
            self.V(rnorm)
            r_done = dve.n

            isc_mm, z_done = {}, {}
            for dc in range(DC):
                bank = dc % 2

                def iscmm(t, dc=dc, bank=bank):
                    if dc == 0:
                        t.wait_ge(dve.h, r_done)
                    if dc >= 2:
                        t.wait_ge(dve.h, z_done[dc - 2])
                    ins = None
                    for hf in range(2):
                        ins = t.matmul(f_ps[bank][:, hf * 512:(hf + 1) * 512],
                                       ics_sb[:, dc * 128:(dc + 1) * 128],
                                       r_sb[:, hf * 512:(hf + 1) * 512], start=True, stop=True)
                    pe.inc(ins)
                self.T(iscmm)
                isc_mm[dc] = pe.n

                def zmul(v, dc=dc, bank=bank):
                    v.wait_ge(pe.h, isc_mm[dc])
                    dve.inc(v.tensor_tensor(self.big1[:, dc * TOK:(dc + 1) * TOK],
                                            f_ps[bank][:], self.big2[:, dc * TOK:(dc + 1) * TOK],
                                            ALU.mult))
                self.V(zmul)
                z_done[dc] = dve.n
            zT_done = dve.n

            wf_loaded, sign_done, osc_done = {}, {}, {}
            fmm_done, ep1_done, ep2_done, ep3_done = {}, {}, {}, {}
            x2_loaded, ystore = {}, {}

            def x2first(s):
                s.wait_ge(dve.h, zT_done)
                for b in range(2):
                    qs.inc(s.dma_start(out=x2c[b][:],
                                       in_=self.x2_dram.ap()[b * 128:(b + 1) * 128, :]), 16)
                    x2_loaded[b] = qs.n
            self.S(x2first)

            for oc in range(DC):
                wp, sp = oc % 2, oc % 2

                def wfl(gp, oc=oc):
                    if oc == 0:
                        gp.wait_ge(dve.h, self.xn2T_done)
                    if oc >= 1:
                        gp.wait_ge(act.h, sign_done[oc - 1])
                    qg.inc(gp.dma_start(out=wf32[:],
                                        in_=self.w_blk["wf"][oc].rearrange("p dc e -> p (dc e)")), 16)
                self.G(wfl)
                wf_loaded[oc] = qg.n

                def sgn(sc, oc=oc, wp=wp):
                    sc.wait_ge(qg.h, wf_loaded[oc])
                    if oc >= 2:
                        sc.wait_ge(pe.h, fmm_done[oc - 2])
                    act.inc(sc.sign(swf[wp][:], wf32[:]))
                self.A(sgn)
                sign_done[oc] = act.n

                def oscmm(t, oc=oc):
                    if oc == 0:
                        t.wait_ge(dve.h, zT_done)
                    if oc >= 2:
                        t.wait_ge(act.h, osc_done[oc - 2])
                    if oc >= 1:
                        t.wait_ge(act.h, osc_done[oc - 1])
                    ins = None
                    for hf in range(2):
                        ins = t.matmul(aux_ps[:, hf * 512:(hf + 1) * 512],
                                       ocs_sb[:, oc * 128:(oc + 1) * 128],
                                       r_sb[:, hf * 512:(hf + 1) * 512], start=True, stop=True)
                    pe.inc(ins)
                self.T(oscmm)
                osc_mm = pe.n

                def oscev(sc, oc=oc, sp=sp, osc_mm=osc_mm):
                    sc.wait_ge(pe.h, osc_mm)
                    if oc >= 2:
                        sc.wait_ge(dve.h, ep1_done[oc - 2])
                    act.inc(sc.activation(osc_sb[sp][:], aux_ps[:], AF.Copy))
                self.A(oscev)
                osc_done[oc] = act.n

                def fmm(t, oc=oc, wp=wp, sp=sp):
                    t.wait_ge(act.h, sign_done[oc])
                    if oc == 0:
                        t.wait_ge(dve.h, zT_done)
                    if oc >= 2:
                        t.wait_ge(dve.h, ep1_done[oc - 2])
                    ins = None
                    for hc in range(DC):
                        for hf in range(2):
                            ins = t.matmul(f_ps[sp][:, hf * 512:(hf + 1) * 512],
                                           swf[wp][:, hc * 128:(hc + 1) * 128],
                                           self.big1[:, hc * TOK + hf * 512:hc * TOK + (hf + 1) * 512],
                                           start=(hc == 0), stop=(hc == DC - 1))
                    pe.inc(ins)
                self.T(fmm)
                fmm_done[oc] = pe.n

                def ep1(v, oc=oc, sp=sp):
                    v.wait_ge(pe.h, fmm_done[oc])
                    v.wait_ge(act.h, osc_done[oc])
                    if oc >= 1:
                        v.wait_ge(dve.h, ep3_done[oc - 1])
                    dve.inc(v.tensor_tensor(tch[:], f_ps[sp][:], osc_sb[sp][:], ALU.mult))
                self.V(ep1)
                ep1_done[oc] = dve.n

                def ep3(v, oc=oc, sp=sp):
                    v.wait_ge(qs.h, x2_loaded[oc])
                    if oc >= 1:
                        v.wait_ge(qs.h, ystore[oc - 1])
                    dve.inc(v.scalar_tensor_tensor(outb[sp][:], tch[:],
                                                   bias_cols[:, oc:oc + 1], x2c[sp][:],
                                                   ALU.add, ALU.add))
                self.V(ep3)
                ep3_done[oc] = dve.n

                def yst(s, oc=oc, sp=sp):
                    s.wait_ge(dve.h, ep3_done[oc])
                    qs.inc(s.dma_start(out=self.y_fm[oc * 128:(oc + 1) * 128, :],
                                       in_=outb[sp][:]), 16)
                    if oc + 2 < DC:
                        qs.inc(s.dma_start(out=x2c[sp][:],
                                           in_=self.x2_dram.ap()[(oc + 2) * 128:(oc + 3) * 128, :]), 16)
                self.S(yst)
                ystore[oc] = qs.n - (16 if oc + 2 < DC else 0)
                if oc + 2 < DC:
                    x2_loaded[oc + 2] = qs.n

            self._dump("zT", self.big1[:])
            self._dump("r8", r_sb[:])
            self._dump("et8", et_sb[:])
            self._dump("rsrep", rs_rep[:])
            self._dump("bias_cols", bias_cols[:])
            self._dump("osc_last", osc_sb[(DC - 1) % 2][:])
            self._dump("tch_last", tch[:])

    def finalize(self):
        qs, qa, qg = self.qs, self.qa, self.qg

        def fin(s):
            s.wait_ge(qa.h, qa.n)
            s.wait_ge(qg.h, qg.n)
            s.wait_ge(qs.h, qs.n)
        self.S(fin)

        def fing(gp):
            gp.wait_ge(qs.h, qs.n)
        self.G(fing)


def build_nc(debug_dumps=(), phases=5):
    return Builder(debug_dumps, phases).build()


def _prep_inputs(L0_attn_x, L0_attn_w_q, L0_attn_w_k, L0_attn_w_v, L0_attn_w_o,
                 L0_ffn_weight, L0_ffn_bias, L0_ffn_gate_w,
                 L0_ffn_in_channel_scale, L0_ffn_out_channel_scale):
    x = np.ascontiguousarray(np.asarray(L0_attn_x, dtype=np.float32))
    wq = np.asarray(L0_attn_w_q, dtype=np.float32)
    wk = np.asarray(L0_attn_w_k, dtype=np.float32)
    wv = np.asarray(L0_attn_w_v, dtype=np.float32)
    wo = np.asarray(L0_attn_w_o, dtype=np.float32)
    wf = np.asarray(L0_ffn_weight, dtype=np.float32)
    gw = np.asarray(L0_ffn_gate_w, dtype=np.float32)
    bias = np.asarray(L0_ffn_bias, dtype=np.float32)
    ics = np.ascontiguousarray(np.asarray(L0_ffn_in_channel_scale, dtype=np.float32))
    ocs = np.ascontiguousarray(np.asarray(L0_ffn_out_channel_scale, dtype=np.float32))

    shared = {
        "wq_blk": _blocked_T(wq), "wk_blk": _blocked_T(wk), "wv_blk": _blocked_T(wv),
        "wo_blk": _blocked_T(wo), "wf_blk": _blocked_T(wf),
        "gwt_h": np.ascontiguousarray(
            gw.T.reshape(DC, 128, NE).transpose(1, 0, 2).reshape(128, DC * NE)),
        "bias32": np.ascontiguousarray(bias.reshape(32, 128)),
        "ics8": ics, "ocs8": ocs,
    }
    xf = x.reshape(2 * 4096, D)
    in_maps = []
    for c in range(NCORE):
        m = dict(shared)
        m["x_t"] = np.ascontiguousarray(xf[c * TOK:(c + 1) * TOK, :])
        in_maps.append(m)
    return in_maps


_NC_CACHE = {}
PHASES = 5


def kernel(**inputs):
    in_maps = _prep_inputs(**inputs)
    key = (tuple(DEBUG_DUMPS), PHASES)
    if key not in _NC_CACHE:
        _NC_CACHE[key] = build_nc(key[0], PHASES)
    nc = _NC_CACHE[key]
    res = run_bass_kernel_spmd(nc, in_maps, core_ids=list(range(NCORE)))
    outs = [np.ascontiguousarray(np.asarray(res.results[c]["y_fm"]).T) for c in range(NCORE)]
    full = np.concatenate(outs, axis=0).reshape(2, 4096, D)
    if DEBUG_DUMPS:
        kernel.last_results = res
    return full


# version 06 (final)
# speedup vs baseline: 1.0278x; 1.0278x over previous
import contextlib
import numpy as np
import ml_dtypes
import concourse.bass as bass
import concourse.mybir as mybir
from concourse.bass_utils import run_bass_kernel_spmd

F32 = mybir.dt.float32
BF16 = mybir.dt.bfloat16
AF = mybir.ActivationFunctionType
ALU = mybir.AluOpType
AX = mybir.AxisListType

NCORE = 8
TOK = 1024
D = 4096
NH = 32
DK = 128
NE = 8
DC = 32
TS = 8
INV_SQRT_DK = 0.08838834764831845
EPS = 1e-5

DEBUG_DUMPS = []
_DBG_SPECS = {
    "xnT": ([128, DC * TOK], BF16),
    "k_own": ([2048, TOK], BF16),
    "q_own": ([D, TOK], BF16),
    "kg0": ([4 * 2049, TOK], BF16),
    "vg0": ([4 * 16 * TOK, DK], BF16),
    "o_fm": ([128, DC * TOK], BF16),
    "x2": ([D, TOK], BF16),
    "xn2T": ([128, DC * TOK], BF16),
    "zT": ([128, DC * TOK], BF16),
    "r8": ([NE, TOK], BF16),
    "et8": ([NE, TOK], F32),
    "rsrep": ([128, TOK], F32),
    "bias_cols": ([128, DC], F32),
    "osc_last": ([128, TOK], BF16),
    "tch_last": ([128, TOK], F32),
}


class Sem:
    def __init__(self, h):
        self.h = h
        self.n = 0

    def inc(self, ins, k=1):
        ins.then_inc(self.h, k)
        self.n += k
        return ins


def _blocked_T(w):
    return np.ascontiguousarray(w.reshape(DC, 128, DC, 128).transpose(0, 3, 2, 1))


class Builder:
    def __init__(self, debug_dumps=(), phases=5):
        self.phases = phases
        self.debug_dumps = list(debug_dumps)
        nc = bass.Bass("TRN2", target_bir_lowering=False)
        self.nc = nc
        self.x_t = nc.dram_tensor("x_t", [TOK, D], F32, kind="ExternalInput").ap()
        self.w_blk = {}
        for nm in ("wq", "wk", "wv", "wo", "wf"):
            self.w_blk[nm] = nc.dram_tensor(nm + "_blk", [DC, 128, DC, 128], F32,
                                            kind="ExternalInput").ap()
        self.gwt_h = nc.dram_tensor("gwt_h", [128, DC * NE], F32, kind="ExternalInput").ap()
        self.bias32 = nc.dram_tensor("bias32", [32, 128], F32, kind="ExternalInput").ap()
        self.ics8 = nc.dram_tensor("ics8", [NE, D], F32, kind="ExternalInput").ap()
        self.ocs8 = nc.dram_tensor("ocs8", [NE, D], F32, kind="ExternalInput").ap()
        self.y_fm = nc.dram_tensor("y_fm", [D, TOK], F32, kind="ExternalOutput").ap()

        self.k_in = [nc.dram_tensor(f"k_in{i}", [2049, TOK], BF16) for i in range(2)]
        self.k_g = [nc.dram_tensor(f"k_g{i}", [4 * 2049, TOK], BF16) for i in range(2)]
        self.v_in = [nc.dram_tensor(f"v_in{i}", [16 * TOK, DK], BF16) for i in range(2)]
        self.v_g = [nc.dram_tensor(f"v_g{i}", [4 * 16 * TOK, DK], BF16) for i in range(2)]
        self.q_sp = nc.dram_tensor("q_sp", [D, TOK], BF16)
        self.xnT_dram = nc.dram_tensor("xnT_dram", [D, TOK], BF16)
        self.stdmean_dram = nc.dram_tensor("stdmean_dram", [16, 128], F32)
        self.x2_dram = nc.dram_tensor("x2_dram", [D, TOK], BF16)
        self.ident128 = nc.inline_tensor(np.eye(128, dtype=np.float32), "ident128")
        self.ones128_bf = nc.inline_tensor(np.ones((128, 128), dtype=ml_dtypes.bfloat16), "ones128bf")
        self.ones128_f = nc.inline_tensor(np.ones((128, 128), dtype=np.float32), "ones128f")
        self.dbg = {}
        for name in self.debug_dumps:
            shape, dt = _DBG_SPECS[name]
            self.dbg[name] = nc.dram_tensor("dbg_" + name, shape, dt, kind="ExternalOutput").ap()

    def T(self, f):
        self.blk.tensor(f)

    def V(self, f):
        self.blk.vector(f)

    def A(self, f):
        self.blk.scalar(f)

    def S(self, f):
        self.blk.sync(f)

    def G(self, f):
        self.blk.gpsimd(f)

    def _dump(self, name, src):
        if name not in self.dbg:
            return
        dst = self.dbg[name]
        qg, qs, qa, pe, act, dve = self.qg, self.qs, self.qa, self.pe, self.act, self.dve

        cc = self.cc

        def f(gp, dst=dst, src=src):
            gp.wait_ge(qs.h, qs.n)
            gp.wait_ge(qa.h, qa.n)
            gp.wait_ge(cc.h, cc.n)
            gp.wait_ge(pe.h, pe.n)
            gp.wait_ge(act.h, act.n)
            gp.wait_ge(dve.h, dve.n)
            qg.inc(gp.dma_start(out=dst, in_=src), 16)
        self.G(f)

    def build(self):
        nc = self.nc
        with (
            nc.sbuf_tensor("big1", [128, DC * TOK], BF16) as big1,
            nc.sbuf_tensor("big2", [128, DC * TOK], BF16) as big2,
            nc.sbuf_tensor("wbuf0", [128, D], BF16) as wbuf0,
            nc.sbuf_tensor("wbuf1", [128, D], BF16) as wbuf1,
            nc.sbuf_tensor("stage0", [128, TOK], BF16) as stage0,
            nc.sbuf_tensor("stage1", [128, TOK], BF16) as stage1,
            nc.sbuf_tensor("ones_bf", [128, 128], BF16) as ones_bf,
            nc.sbuf_tensor("ones_f", [128, 128], F32) as ones_f,
            nc.sbuf_tensor("ident_f", [128, 128], F32) as ident_f,
            nc.sbuf_tensor("smalls", [128, 64], F32) as smalls,
            nc.sbuf_tensor("stdmean", [128, 16], F32) as stdmean,
            nc.sbuf_tensor("norms", [128, 64], F32) as norms,
            nc.semaphore("qs") as _qs,
            nc.semaphore("qa") as _qa,
            nc.semaphore("qg") as _qg,
            nc.semaphore("cc") as _cc,
            nc.semaphore("pe") as _pe,
            nc.semaphore("act") as _act,
            nc.semaphore("dve") as _dve,
            nc.Block() as blk,
        ):
            self.blk = blk
            self.big1, self.big2 = big1, big2
            self.wbuf = [wbuf0, wbuf1]
            self.stage = [stage0, stage1]
            self.ones_bf_sb, self.ones_f_sb, self.ident_f_sb = ones_bf, ones_f, ident_f
            self.smalls = smalls
            self.stdmean_sb = stdmean
            self.norms_sb = norms
            self.qs, self.qa, self.qg, self.cc = Sem(_qs), Sem(_qa), Sem(_qg), Sem(_cc)
            self.pe, self.act, self.dve = Sem(_pe), Sem(_act), Sem(_dve)

            self.p0_consts()
            self.p1_ln1()
            if self.phases >= 2:
                self.p2_proj()
            if self.phases >= 3:
                self.p3_attn()
            if self.phases >= 4:
                self.p4_wo_ln2()
            if self.phases >= 5:
                self.p5_ffn()
            self.finalize()
        return nc

    def p0_consts(self):
        qs, dve = self.qs, self.dve
        sm = self.smalls
        self.C_EPS, self.C_NEGHL = 62, 63

        def f(s):
            qs.inc(s.dma_start(out=self.ones_bf_sb[:], in_=self.ones128_bf.ap()), 16)
            qs.inc(s.dma_start(out=self.ones_f_sb[:], in_=self.ones128_f.ap()), 16)
            qs.inc(s.dma_start(out=self.ident_f_sb[:], in_=self.ident128.ap()), 16)
        self.S(f)

        def cv(v):
            v.memset(sm[:, self.C_EPS:self.C_EPS + 1], EPS)
            dve.inc(v.memset(sm[:, self.C_NEGHL:self.C_NEGHL + 1], -0.5 * float(np.log(128.0))))
        self.V(cv)

    def p1_ln1(self):
        nc, qs, act, dve = self.nc, self.qs, self.act, self.dve
        sm = self.smalls
        M_SUM, M_MEAN, M_MSQS, M_MSQ, M_TMP, M_VAR, M_LN, M_RSTD, M_NEGMR = range(9)
        with contextlib.ExitStack() as _st:
            xt0 = _st.enter_context(nc.sbuf_tensor("xt0", [128, D], F32))
            xt1 = _st.enter_context(nc.sbuf_tensor("xt1", [128, D], F32))
            xn_ts0 = _st.enter_context(nc.sbuf_tensor("xn_ts0", [128, D], BF16))
            sqscr = _st.enter_context(nc.sbuf_tensor("sqscr", [128, D], BF16))
            xt = [xt0, xt1]
            xn_ts = [xn_ts0, xn_ts0]
            x_loaded = {}
            tr_done = {}

            def load0(s):
                for t in range(2):
                    qs.inc(s.dma_start(out=xt[t][:], in_=self.x_t[t * 128:(t + 1) * 128, :]), 16)
                    x_loaded[t] = qs.n
            self.S(load0)

            for ts in range(TS):
                b = ts % 2

                def stats_v(v, ts=ts, b=b):
                    v.wait_ge(qs.h, x_loaded[ts])
                    dve.inc(v.reduce_sum(sm[:, M_SUM:M_SUM + 1], xt[b][:], axis=AX.X))
                    v.wait_ge(dve.h, dve.n)
                    dve.inc(v.tensor_scalar(sm[:, M_MEAN:M_MEAN + 1], sm[:, M_SUM:M_SUM + 1],
                                            1.0 / D, None, ALU.mult))
                self.V(stats_v)

                def stats_a(sc, ts=ts, b=b):
                    sc.wait_ge(qs.h, x_loaded[ts])
                    act.inc(sc.activation(sqscr[:], xt[b][:], AF.Square,
                                          accum_out=sm[:, M_MSQS:M_MSQS + 1]))
                self.A(stats_a)
                sq_ready = act.n

                def var_v(v, sq_ready=sq_ready):
                    v.wait_ge(act.h, sq_ready)
                    dve.inc(v.tensor_scalar(sm[:, M_MSQ:M_MSQ + 1], sm[:, M_MSQS:M_MSQS + 1],
                                            1.0 / D, None, ALU.mult))
                    dve.inc(v.tensor_tensor(sm[:, M_TMP:M_TMP + 1], sm[:, M_MEAN:M_MEAN + 1],
                                            sm[:, M_MEAN:M_MEAN + 1], ALU.mult))
                    v.wait_ge(dve.h, dve.n)
                    dve.inc(v.tensor_tensor(sm[:, M_VAR:M_VAR + 1], sm[:, M_MSQ:M_MSQ + 1],
                                            sm[:, M_TMP:M_TMP + 1], ALU.subtract))
                self.V(var_v)
                var_ready = dve.n

                def rstd_a(sc, ts=ts, var_ready=var_ready):
                    sc.wait_ge(dve.h, var_ready)
                    act.inc(sc.activation(sm[:, M_LN:M_LN + 1], sm[:, M_VAR:M_VAR + 1], AF.Ln,
                                          bias=sm[:, self.C_EPS:self.C_EPS + 1]))
                    sc.wait_ge(act.h, act.n)
                    sc.activation(self.stdmean_sb[:, 8 + ts:9 + ts], sm[:, M_LN:M_LN + 1],
                                  AF.Exp, scale=0.5)
                    act.inc(sc.activation(sm[:, M_RSTD:M_RSTD + 1], sm[:, M_LN:M_LN + 1],
                                          AF.Exp, scale=-0.5))
                self.A(rstd_a)
                rstd_ready = act.n

                def norm_v(v, ts=ts, b=b, rstd_ready=rstd_ready):
                    v.wait_ge(act.h, rstd_ready)
                    if ts >= 1:
                        v.wait_ge(qs.h, tr_done[ts - 1])
                    dve.inc(v.scalar_tensor_tensor(sm[:, M_NEGMR:M_NEGMR + 1],
                                                   sm[:, M_MEAN:M_MEAN + 1],
                                                   -1.0, sm[:, M_RSTD:M_RSTD + 1],
                                                   ALU.mult, ALU.mult))
                    v.tensor_copy(self.stdmean_sb[:, ts:ts + 1], sm[:, M_MEAN:M_MEAN + 1])
                    v.wait_ge(dve.h, dve.n)
                    dve.inc(v.tensor_scalar(xn_ts[b][:], xt[b][:], sm[:, M_RSTD:M_RSTD + 1],
                                            sm[:, M_NEGMR:M_NEGMR + 1], ALU.mult, ALU.add))
                self.V(norm_v)
                ln_done = dve.n

                def store_s(s, ts=ts, b=b, ln_done=ln_done):
                    s.wait_ge(dve.h, ln_done)
                    dst = self.big1[:].rearrange("p (dc t) -> p dc t", t=TOK)[:, :, ts * 128:(ts + 1) * 128]
                    qs.inc(s.dma_start(out=dst,
                                       in_=xn_ts[b][:].rearrange("p (dc k) -> p dc k", k=128),
                                       transpose=True), 16)
                    if ts + 2 < TS:
                        qs.inc(s.dma_start(out=xt[b][:],
                                           in_=self.x_t[(ts + 2) * 128:(ts + 3) * 128, :]), 16)
                self.S(store_s)
                tr_done[ts] = qs.n - (16 if ts + 2 < TS else 0)
                if ts + 2 < TS:
                    x_loaded[ts + 2] = qs.n

            def spill(s):
                s.wait_ge(dve.h, dve.n)
                s.wait_ge(qs.h, tr_done[TS - 1])
                qs.inc(s.dma_start(
                    out=self.xnT_dram.ap().rearrange("(dc dp) t -> dp dc t", dp=128),
                    in_=self.big1[:].rearrange("p (dc t) -> p dc t", t=TOK)), 16)
            self.S(spill)
            self.xnT_spilled = qs.n

            pe = self.pe
            sm_ps = _st.enter_context(nc.psum_tensor("sm_ps", [16, 128], F32))
            sm_tr = _st.enter_context(nc.sbuf_tensor("sm_tr", [16, 128], F32))

            def smtr(t):
                t.wait_ge(dve.h, dve.n)
                t.wait_ge(act.h, act.n)
                t.wait_ge(qs.h, qs.n)
                pe.inc(t.transpose(sm_ps[0:16, 0:128], self.stdmean_sb[:], self.ident_f_sb[:]))
            self.T(smtr)
            smtr_done = pe.n

            def smev(sc, smtr_done=smtr_done):
                sc.wait_ge(pe.h, smtr_done)
                act.inc(sc.activation(sm_tr[:], sm_ps[0:16, 0:128], AF.Copy))
            self.A(smev)
            smev_done = act.n

            def smst(s, smev_done=smev_done):
                s.wait_ge(act.h, smev_done)
                qs.inc(s.dma_start(out=self.stdmean_dram.ap(), in_=sm_tr[:]), 16)
            self.S(smst)
            self.stdmean_stored = qs.n
            self.p1_act_end = act.n

            def fence(t):
                t.wait_ge(qs.h, qs.n)
            self.T(fence)
        self.xnT_done = self.xnT_spilled
        self._dump("xnT", self.big1[:])

    def p2_proj(self):
        nc = self.nc
        qs, qa, qg, cc, pe, act = self.qs, self.qa, self.qg, self.cc, self.pe, self.act
        dve = self.dve
        with contextlib.ExitStack() as _st:
            proj_ps0 = _st.enter_context(nc.psum_tensor("proj_ps0", [128, TOK], F32))
            proj_ps1 = _st.enter_context(nc.psum_tensor("proj_ps1", [128, TOK], F32))
            norm_ps = _st.enter_context(nc.psum_tensor("norm_ps", [128, TOK], F32))
            vt_stage = _st.enter_context(nc.sbuf_tensor("vt_stage", [128, TOK], BF16))
            sqst0 = _st.enter_context(nc.sbuf_tensor("sqst0", [128, TOK], BF16))
            sqst1 = _st.enter_context(nc.sbuf_tensor("sqst1", [128, TOK], BF16))
            proj_ps = [proj_ps0, proj_ps1]
            mm_done, evac_done, store_done, w_loaded = {}, {}, {}, {}
            nmm_done, nred_done = {}, {}
            self.ag_done = {}

            mats = [("k", self.w_blk["wk"]), ("v", self.w_blk["wv"]), ("q", self.w_blk["wq"])]
            for mi, (mname, wsrc) in enumerate(mats):
                gi = mi * DC
                for ec in range(DC):
                    g = gi + ec
                    wp, sp = g % 2, g % 2

                    def wload(gp, g=g, ec=ec, wp=wp, wsrc=wsrc):
                        if g >= 2:
                            gp.wait_ge(pe.h, mm_done[g - 2])
                        qg.inc(gp.dma_start(out=self.wbuf[wp][:],
                                            in_=wsrc[ec].rearrange("p dc e -> p (dc e)")), 16)
                    self.G(wload)
                    w_loaded[g] = qg.n

                    def mm(t, g=g, wp=wp, sp=sp):
                        t.wait_ge(qg.h, w_loaded[g])
                        if g == 0:
                            t.wait_ge(qs.h, self.xnT_done)
                            t.wait_ge(act.h, self.p1_act_end)
                        if g >= 2:
                            t.wait_ge(act.h, evac_done[g - 2])
                        ins = None
                        for dc in range(DC):
                            for hf in range(2):
                                ins = t.matmul(proj_ps[sp][:, hf * 512:(hf + 1) * 512],
                                               self.wbuf[wp][:, dc * 128:(dc + 1) * 128],
                                               self.big1[:, dc * TOK + hf * 512:dc * TOK + (hf + 1) * 512],
                                               start=(dc == 0), stop=(dc == DC - 1))
                        pe.inc(ins)
                    self.T(mm)
                    mm_done[g] = pe.n

                    def evac(sc, g=g, sp=sp, mname=mname):
                        sc.wait_ge(pe.h, mm_done[g])
                        if g >= 2:
                            sc.wait_ge(qs.h, store_done[g - 2])
                        act.inc(sc.activation(self.stage[sp][:], proj_ps[sp][:], AF.Copy))
                        if mname in ("k", "q"):
                            if g >= 2 and (g - 2) in nmm_done:
                                sc.wait_ge(pe.h, nmm_done[g - 2])
                            act.inc(sc.activation([sqst0, sqst1][sp][:], self.stage[sp][:],
                                                  AF.Square))
                    self.A(evac)
                    evac_done[g] = act.n

                    if mname in ("k", "q"):
                        def nmm(t, g=g, ec=ec, sp=sp):
                            t.wait_ge(act.h, evac_done[g])
                            if g >= 2 and (g - 2) in nred_done:
                                t.wait_ge(dve.h, nred_done[g - 2])
                            ins = None
                            for hf in range(2):
                                ins = t.matmul(norm_ps[:, hf * 512:(hf + 1) * 512],
                                               self.ones_bf_sb[:],
                                               [sqst0, sqst1][sp][:, hf * 512:(hf + 1) * 512],
                                               start=True, stop=True)
                            pe.inc(ins)
                        self.T(nmm)
                        nmm_done[g] = pe.n

                        def nred(v, g=g, ec=ec, mname=mname):
                            v.wait_ge(pe.h, nmm_done[g])
                            col = ec if mname == "q" else 32 + ec
                            dve.inc(v.reduce_max(self.norms_sb[:, col:col + 1],
                                                 norm_ps[:], axis=AX.X))
                        self.V(nred)
                        nred_done[g] = dve.n

                    if mname == "k":
                        def store(s, g=g, ec=ec, sp=sp):
                            s.wait_ge(act.h, evac_done[g])
                            half, row = (0, ec * 128) if ec < 16 else (1, (ec - 16) * 128)
                            qs.inc(s.dma_start(out=self.k_in[half].ap()[row:row + 128, :],
                                               in_=self.stage[sp][:]), 16)
                        self.S(store)
                        store_done[g] = qs.n
                    elif mname == "v":
                        def vtr(sc, g=g, ec=ec, sp=sp):
                            sc.wait_ge(act.h, evac_done[g])
                            if ec >= 1:
                                sc.wait_ge(qs.h, store_done[g - 1])
                            qa.inc(sc.dma_start(
                                out=vt_stage[:].rearrange("p (ts v) -> p ts v", v=128),
                                in_=self.stage[sp][:].rearrange("p (ts k) -> p ts k", k=128),
                                transpose=True), 16)
                        self.A(vtr)
                        vtr_n = qa.n

                        def store(s, g=g, ec=ec, vtr_n=vtr_n):
                            s.wait_ge(qa.h, vtr_n)
                            half, row = (0, ec * TOK) if ec < 16 else (1, (ec - 16) * TOK)
                            qs.inc(s.dma_start(
                                out=self.v_in[half].ap()[row:row + TOK, :]
                                    .rearrange("(p ts) v -> p ts v", ts=8),
                                in_=vt_stage[:].rearrange("p (ts v) -> p ts v", v=128)), 16)
                        self.S(store)
                        store_done[g] = qs.n
                    else:
                        def store(s, g=g, ec=ec, sp=sp):
                            s.wait_ge(act.h, evac_done[g])
                            qs.inc(s.dma_start(out=self.q_sp.ap()[ec * 128:(ec + 1) * 128, :],
                                               in_=self.stage[sp][:]), 16)
                        self.S(store)
                        store_done[g] = qs.n


            self.q_store_done = store_done[3 * DC - 1]

            def kmaxrow(g):
                g.wait_ge(dve.h, nred_done[31])
                qg.inc(g.dma_start(out=self.k_in[0].ap()[2048:2049, 0:16],
                                   in_=self.norms_sb[0:1, 32:48]), 16)
                qg.inc(g.dma_start(out=self.k_in[1].ap()[2048:2049, 0:16],
                                   in_=self.norms_sb[0:1, 48:64]), 16)
            self.G(kmaxrow)

            def ags(gp):
                gp.wait_ge(qs.h, qs.n)
                gp.wait_ge(qa.h, qa.n)
                gp.wait_ge(qg.h, qg.n)
                for mname, src_t, dst_t in (("k", self.k_in, self.k_g),
                                            ("v", self.v_in, self.v_g)):
                    for half in range(2):
                        cc.inc(gp.collective_compute(
                            "AllGather", ALU.bypass,
                            replica_groups=[[0, 1, 2, 3], [4, 5, 6, 7]],
                            ins=[src_t[half].ap().opt()],
                            outs=[dst_t[half].ap().opt()]))
                        self.ag_done[(mname, half)] = cc.n
            self.G(ags)

            def fence(t):
                t.wait_ge(qs.h, qs.n)
                t.wait_ge(qa.h, qa.n)
            self.T(fence)
        self._dump("k_own", self.k_in[0].ap())
        self._dump("q_own", self.q_sp.ap())
        self._dump("kg0", self.k_g[0].ap())
        self._dump("vg0", self.v_g[0].ap())

    def p3_attn(self):
        nc = self.nc
        qs, qg, cc, pe, act, dve = self.qs, self.qg, self.cc, self.pe, self.act, self.dve
        sm = self.smalls
        C_LP, C_LSUM, C_RL = 44, 60, 61
        with contextlib.ExitStack() as _st:
            k_h0 = _st.enter_context(nc.sbuf_tensor("k_h0", [128, 4096], BF16))
            k_h1 = _st.enter_context(nc.sbuf_tensor("k_h1", [128, 4096], BF16))
            v_h = _st.enter_context(nc.sbuf_tensor("v_h", [128, 4096], BF16))
            q_h0 = _st.enter_context(nc.sbuf_tensor("q_h0", [128, TOK], BF16))
            q_h1 = _st.enter_context(nc.sbuf_tensor("q_h1", [128, TOK], BF16))
            km4 = _st.enter_context(nc.sbuf_tensor("km4", [1, 128], F32))
            cneg_all = _st.enter_context(nc.sbuf_tensor("cneg_all", [128, 32], F32))
            at_tm0 = _st.enter_context(nc.sbuf_tensor("at_tm0", [128, 4096], BF16))
            at_tm1 = _st.enter_context(nc.sbuf_tensor("at_tm1", [128, 4096], BF16))
            at_tm2, at_tm3 = self.wbuf
            s_ps0 = _st.enter_context(nc.psum_tensor("s_ps0", [128, TOK], F32))
            s_ps1 = _st.enter_context(nc.psum_tensor("s_ps1", [128, TOK], F32))
            o_ps = _st.enter_context(nc.psum_tensor("o_ps", [128, TOK], F32))
            c_ps = _st.enter_context(nc.psum_tensor("c_ps", [128, 512], F32))
            at_tm = [at_tm0, at_tm1, at_tm2, at_tm3]
            s_ps = [s_ps0, s_ps1]
            k_hb = [k_h0, k_h1]
            q_hb = [q_h0, q_h1]
            kq_loaded = {}
            v_loaded = {}
            s_done = {}
            exp_done = {}
            exp_qt_done = {}
            scale_done = {}
            tr_done = {}
            av_done = {}
            av_pred = {}
            oevac_done = {}

            def emit_kq_load(g, h):
                half, hh = (0, h) if h < 16 else (1, h - 16)
                kg = self.k_g[half].ap()
                kb = k_hb[h % 2]
                for r in range(4):
                    qg.inc(g.dma_start(
                        out=kb[:, r * 1024:(r + 1) * 1024],
                        in_=kg[r * 2049 + hh * 128: r * 2049 + hh * 128 + 128, :]), 16)
                qg.inc(g.dma_start(out=q_hb[h % 2][:],
                                   in_=self.q_sp.ap()[h * 128:(h + 1) * 128, :]), 16)

            def emit_v_load(g, h):
                half, hh = (0, h) if h < 16 else (1, h - 16)
                vg = self.v_g[half].ap()
                for r in range(4):
                    src = vg[r * 16 * TOK + hh * TOK: r * 16 * TOK + (hh + 1) * TOK, :]
                    qg.inc(g.dma_start(
                        out=v_h[:, r * 1024:(r + 1) * 1024].rearrange("p (ts v) -> p ts v", v=128),
                        in_=src.rearrange("(p ts) v -> p ts v", ts=8)), 16)

            def prol_g(g):
                g.wait_ge(qs.h, self.q_store_done)
                g.wait_ge(cc.h, self.ag_done[("v", 1)])
                for half in range(2):
                    kg = self.k_g[half].ap()
                    for r in range(4):
                        qg.inc(g.dma_start(
                            out=km4[0:1, half * 64 + r * 16: half * 64 + r * 16 + 16],
                            in_=kg[r * 2049 + 2048: r * 2049 + 2049, 0:16]), 16)
                emit_kq_load(g, 0)
                emit_v_load(g, 0)
                emit_kq_load(g, 1)
            self.G(prol_g)
            km_loaded = qg.n - 9 * 16 - 4 * 16
            kq_loaded[0] = qg.n - 9 * 16
            v_loaded[0] = qg.n - 5 * 16
            kq_loaded[1] = qg.n

            def prol_v(v):
                v.wait_ge(qg.h, km_loaded)
                for half in range(2):
                    seg = km4[0:1, half * 64:(half + 1) * 64]
                    dve.inc(v.reduce_max(
                        sm[0:1, half * 16:(half + 1) * 16],
                        seg.rearrange("p (r hh) -> p hh r", hh=16), axis=AX.X))
                v.wait_ge(dve.h, dve.n)
                dve.inc(v.tensor_tensor(sm[0:1, 0:32], sm[0:1, 0:32],
                                        self.norms_sb[0:1, 0:32], ALU.mult))
            self.V(prol_v)
            prod_done = dve.n

            def prol_a(sc, prod_done=prod_done):
                sc.wait_ge(dve.h, prod_done)
                act.inc(sc.activation(sm[0:1, 0:32], sm[0:1, 0:32], AF.Ln))
                sc.wait_ge(act.h, act.n)
                act.inc(sc.activation(sm[0:1, 0:32], sm[0:1, 0:32],
                                      AF.Exp, scale=0.5,
                                      bias=sm[0:1, self.C_NEGHL:self.C_NEGHL + 1]))
            self.A(prol_a)
            c_act = act.n

            def prol_v2(v, c_act=c_act):
                v.wait_ge(act.h, c_act)
                dve.inc(v.tensor_scalar(sm[0:1, 0:32], sm[0:1, 0:32],
                                        -1.0, None, ALU.mult))
            self.V(prol_v2)
            cneg_row = dve.n

            def prol_t(t, cneg_row=cneg_row):
                t.wait_ge(dve.h, cneg_row)
                pe.inc(t.matmul(c_ps[0:128, 0:32], self.ones_f_sb[0:1, :],
                                sm[0:1, 0:32], start=True, stop=True))
            self.T(prol_t)
            cbc = pe.n

            def prol_a2(sc, cbc=cbc):
                sc.wait_ge(pe.h, cbc)
                act.inc(sc.activation(cneg_all[:], c_ps[0:128, 0:32], AF.Copy))
            self.A(prol_a2)
            cneg_done = act.n

            for h in range(NH):
                act0 = act.n
                for qt in range(TS):
                    for kc in range(4):
                        exp_done[h * 32 + qt * 4 + kc] = act0 + qt * 4 + kc + 1
                    exp_qt_done[(h, qt)] = act0 + qt * 4 + 4
                qs0, qa0, dve0 = qs.n, self.qa.n, dve.n
                for k, qt in enumerate((0, 1, 2, 3, 4, 6)):
                    tr_done[h * TS + qt] = (qs.h, qs0 + (k + 1) * 16)
                for k, qt in enumerate((5, 7)):
                    tr_done[h * TS + qt] = (self.qa.h, qa0 + (k + 1) * 16)

                for qt in range(TS):
                    gq = h * TS + qt
                    for kc in range(4):
                        gsc = gq * 4 + kc
                        sp = gsc % 2

                        def smm(t, h=h, qt=qt, kc=kc, sp=sp, gsc=gsc):
                            if gsc == 0:
                                t.wait_ge(qg.h, kq_loaded[0])
                                t.wait_ge(act.h, cneg_done)
                            if gsc >= 2:
                                t.wait_ge(act.h, exp_done[gsc - 2])
                            if kc == 0 and qt == 0:
                                t.wait_ge(qg.h, kq_loaded[h])
                            ins = None
                            for hf in range(2):
                                col = kc * 1024 + hf * 512
                                ins = t.matmul(s_ps[sp][:, hf * 512:(hf + 1) * 512],
                                               q_hb[h % 2][:, qt * 128:(qt + 1) * 128],
                                               k_hb[h % 2][:, col:col + 512],
                                               start=True, stop=True)
                            pe.inc(ins)
                        self.T(smm)
                        s_done[gsc] = pe.n

                def head_act(sc, h=h, dve0=dve0):
                    def issue_tra(qt):
                        sc.wait_ge(dve.h, dve0 + 3 * (qt + 1))
                        if h >= 1 and qt == 5:
                            sc.wait_ge(pe.h, av_done[h - 1])
                        gq2 = h * TS + qt
                        dstv2 = self.big1[:].rearrange("p (kt q) -> p kt q", q=TOK)[:, :, qt * 128:(qt + 1) * 128]
                        self.qa.inc(sc.dma_start(
                            out=dstv2,
                            in_=at_tm[gq2 % 4][:].rearrange("p (kt k) -> p kt k", k=128),
                            transpose=True), 16)
                    for qt in range(TS):
                        gq = h * TS + qt
                        tp = gq % 4
                        for kc in range(4):
                            gsc = gq * 4 + kc
                            sc.wait_ge(pe.h, s_done[gsc])
                            if kc == 0 and gq >= 4:
                                semh, val = tr_done[gq - 4]
                                sc.wait_ge(semh, val)
                            lp = C_LP + (qt % 4) * 4 + kc
                            act.inc(sc.activation(at_tm[tp][:, kc * 1024:(kc + 1) * 1024],
                                                  s_ps[gsc % 2][:], AF.Exp, scale=INV_SQRT_DK,
                                                  bias=cneg_all[:, h:h + 1],
                                                  accum_out=sm[:, lp:lp + 1]))
                    issue_tra(5)
                    issue_tra(7)
                self.A(head_act)
                assert act.n == act0 + 32, (act.n, act0)
                assert self.qa.n == qa0 + 32, (self.qa.n, qa0)

                for qt in range(TS):
                    gq = h * TS + qt
                    tp = gq % 4

                    def scl(v, h=h, qt=qt, tp=tp):
                        v.wait_ge(act.h, exp_qt_done[(h, qt)])
                        lp0 = C_LP + (qt % 4) * 4
                        dve.inc(v.reduce_sum(sm[:, C_LSUM:C_LSUM + 1], sm[:, lp0:lp0 + 4],
                                             axis=AX.X))
                        v.wait_ge(dve.h, dve.n)
                        dve.inc(v.reciprocal(sm[:, C_RL:C_RL + 1], sm[:, C_LSUM:C_LSUM + 1]))
                        v.wait_ge(dve.h, dve.n)
                        dve.inc(v.tensor_scalar(at_tm[tp][:], at_tm[tp][:],
                                                sm[:, C_RL:C_RL + 1], None, ALU.mult))
                    self.V(scl)
                    scale_done[(h, qt)] = dve.n
                    assert scale_done[(h, qt)] == dve0 + 3 * (qt + 1)

                    if qt not in (5, 7):
                        dstv = self.big1[:].rearrange("p (kt q) -> p kt q", q=TOK)[:, :, qt * 128:(qt + 1) * 128]

                        def trs(s, h=h, qt=qt, tp=tp, dstv=dstv):
                            s.wait_ge(dve.h, scale_done[(h, qt)])
                            if h >= 1 and qt == 0:
                                s.wait_ge(pe.h, av_done[h - 1])
                            qs.inc(s.dma_start(out=dstv,
                                               in_=at_tm[tp][:].rearrange("p (kt k) -> p kt k", k=128),
                                               transpose=True), 16)
                        self.S(trs)
                        assert tr_done[gq] == (qs.h, qs.n), (tr_done[gq], qs.n)

                if h + 2 < NH:
                    def pf_kq(g, h=h):
                        g.wait_ge(pe.h, s_done[(h * TS + TS - 1) * 4 + 3])
                        emit_kq_load(g, h + 2)
                    self.G(pf_kq)
                    kq_loaded[h + 2] = qg.n

                def av(t, h=h):
                    t.wait_ge(qg.h, v_loaded[h])
                    if h >= 1:
                        t.wait_ge(dve.h, oevac_done[h - 1])
                    ins = None
                    for hf in range(2):
                        for qt in range(hf * 4, hf * 4 + 4):
                            semh, val = tr_done[h * TS + qt]
                            t.wait_ge(semh, val)
                        for kt in range(NH):
                            ins = t.matmul(o_ps[:, hf * 512:(hf + 1) * 512],
                                           v_h[:, kt * 128:(kt + 1) * 128],
                                           self.big1[:, kt * TOK + hf * 512:kt * TOK + (hf + 1) * 512],
                                           start=(kt == 0), stop=(kt == NH - 1))
                    pe.inc(ins)
                self.T(av)
                av_done[h] = pe.n

                def oevac(v, h=h):
                    v.wait_ge(pe.h, av_done[h])
                    dve.inc(v.tensor_copy(self.big2[:, h * TOK:(h + 1) * TOK], o_ps[:]))
                self.V(oevac)
                oevac_done[h] = dve.n

                if h + 1 < NH:
                    def pf_v(g, h=h):
                        g.wait_ge(pe.h, av_done[h])
                        emit_v_load(g, h + 1)
                    self.G(pf_v)
                    v_loaded[h + 1] = qg.n

            self.attn_done_dve = dve.n
            self.attn_done = act.n
            self.attn_qs_done = qs.n
            self.attn_qa_done = self.qa.n
            self.attn_pe_done = pe.n

            def fence(t):
                t.wait_ge(dve.h, self.attn_done_dve)
                t.wait_ge(act.h, self.attn_done)
                t.wait_ge(qs.h, qs.n)
                t.wait_ge(self.qa.h, self.qa.n)
            self.T(fence)
        self._dump("o_fm", self.big2[:])

    def p4_wo_ln2(self):
        nc = self.nc
        qs, qa, qg, pe, act, dve = self.qs, self.qa, self.qg, self.pe, self.act, self.dve
        with contextlib.ExitStack() as _st:
            y_ps0 = _st.enter_context(nc.psum_tensor("y_ps0", [128, TOK], F32))
            y_ps1 = _st.enter_context(nc.psum_tensor("y_ps1", [128, TOK], F32))
            mean_ps = _st.enter_context(nc.psum_tensor("mean_ps", [128, TOK], F32))
            msq_ps = _st.enter_context(nc.psum_tensor("msq_ps", [128, TOK], F32))
            xfm0 = _st.enter_context(nc.sbuf_tensor("xfm0", [128, TOK], BF16))
            xfm1 = _st.enter_context(nc.sbuf_tensor("xfm1", [128, TOK], BF16))
            std_rep = _st.enter_context(nc.sbuf_tensor("std_rep", [128, TOK], F32))
            mean_rep = _st.enter_context(nc.sbuf_tensor("mean_rep", [128, TOK], F32))
            x2c0 = _st.enter_context(nc.sbuf_tensor("x2c0", [128, TOK], BF16))
            x2c1 = _st.enter_context(nc.sbuf_tensor("x2c1", [128, TOK], BF16))
            sqc = _st.enter_context(nc.sbuf_tensor("sqc", [128, TOK], F32))
            rstd_rep = _st.enter_context(nc.sbuf_tensor("rstd_rep", [128, TOK], F32))
            negmr_rep = _st.enter_context(nc.sbuf_tensor("negmr_rep", [128, TOK], F32))
            tchunk = _st.enter_context(nc.sbuf_tensor("tchunk", [128, TOK], F32))
            y_ps = [y_ps0, y_ps1]
            xfm = [xfm0, xfm1]
            x2c = [x2c0, x2c1]
            w_loaded, mm_done, x2_done, sq_done = {}, {}, {}, {}
            stat_done, store_done, xfm_loaded = {}, {}, {}

            def xfm_load(sc, oc, b):
                qa.inc(sc.dma_start(out=xfm[b][:],
                                    in_=self.xnT_dram.ap()[oc * 128:(oc + 1) * 128, :]), 16)

            def first(sc):
                sc.wait_ge(act.h, self.attn_done)
                sc.wait_ge(dve.h, self.attn_done_dve)
                sc.wait_ge(qs.h, self.attn_qs_done)
                sc.wait_ge(qa.h, self.attn_qa_done)
                sc.wait_ge(pe.h, self.attn_pe_done)
                xfm_load(sc, 0, 0)
                xfm_load(sc, 1, 1)
                for ts in range(TS):
                    qa.inc(sc.dma_start(
                        out=mean_rep[:, ts * 128:(ts + 1) * 128],
                        in_=self.stdmean_dram.ap()[ts:ts + 1, :].partition_broadcast(128)), 16)
                    qa.inc(sc.dma_start(
                        out=std_rep[:, ts * 128:(ts + 1) * 128],
                        in_=self.stdmean_dram.ap()[8 + ts:9 + ts, :].partition_broadcast(128)), 16)
            self.A(first)
            xfm_loaded[0] = qa.n - 16 * 17
            xfm_loaded[1] = qa.n - 16 * 16
            self.rep_loaded = qa.n

            for oc in range(DC):
                wp, sp = oc % 2, oc % 2

                def wload(gp, oc=oc, wp=wp):
                    if oc == 0:
                        gp.wait_ge(act.h, self.attn_done)
                        gp.wait_ge(dve.h, self.attn_done_dve)
                        gp.wait_ge(qs.h, self.attn_qs_done)
                        gp.wait_ge(qa.h, self.attn_qa_done)
                        gp.wait_ge(pe.h, self.attn_pe_done)
                    if oc >= 2:
                        gp.wait_ge(pe.h, mm_done[oc - 2])
                    qg.inc(gp.dma_start(out=self.wbuf[wp][:],
                                        in_=self.w_blk["wo"][oc].rearrange("p dc e -> p (dc e)")), 16)
                self.G(wload)
                w_loaded[oc] = qg.n

                def mm(t, oc=oc, wp=wp, sp=sp):
                    t.wait_ge(qg.h, w_loaded[oc])
                    if oc >= 2:
                        t.wait_ge(dve.h, x2_done[oc - 2])
                    ins = None
                    for ec in range(DC):
                        for hf in range(2):
                            ins = t.matmul(y_ps[sp][:, hf * 512:(hf + 1) * 512],
                                           self.wbuf[wp][:, ec * 128:(ec + 1) * 128],
                                           self.big2[:, ec * TOK + hf * 512:ec * TOK + (hf + 1) * 512],
                                           start=(ec == 0), stop=(ec == DC - 1))
                    pe.inc(ins)
                self.T(mm)
                mm_done[oc] = pe.n

                def addres(v, oc=oc, sp=sp):
                    v.wait_ge(pe.h, mm_done[oc])
                    v.wait_ge(qa.h, xfm_loaded[oc])
                    if oc == 0:
                        v.wait_ge(qa.h, self.rep_loaded)
                    if oc >= 2:
                        v.wait_ge(qs.h, store_done[oc - 2])
                        v.wait_ge(pe.h, stat_done[oc - 2])
                    v.tensor_tensor(tchunk[:], xfm[sp][:], std_rep[:], ALU.mult)
                    v.tensor_tensor(tchunk[:], tchunk[:], mean_rep[:], ALU.add)
                    dve.inc(v.tensor_tensor(x2c[sp][:], y_ps[sp][:], tchunk[:], ALU.add))
                self.V(addres)
                x2_done[oc] = dve.n

                def sq(sc, oc=oc, sp=sp):
                    sc.wait_ge(dve.h, x2_done[oc])
                    act.inc(sc.activation(sqc[:], x2c[sp][:], AF.Square))
                    if oc + 2 < DC:
                        xfm_load(sc, oc + 2, sp)
                self.A(sq)
                sq_done[oc] = act.n
                if oc + 2 < DC:
                    xfm_loaded[oc + 2] = qa.n

                def stats(t, oc=oc, sp=sp):
                    t.wait_ge(act.h, sq_done[oc])
                    for hf in range(2):
                        t.matmul(mean_ps[:, hf * 512:(hf + 1) * 512], self.ones_bf_sb[:],
                                 x2c[sp][:, hf * 512:(hf + 1) * 512],
                                 start=(oc == 0), stop=(oc == DC - 1))
                    ins = None
                    for hf in range(2):
                        ins = t.matmul(msq_ps[:, hf * 512:(hf + 1) * 512], self.ones_f_sb[:],
                                       sqc[:, hf * 512:(hf + 1) * 512],
                                       start=(oc == 0), stop=(oc == DC - 1))
                    pe.inc(ins)
                self.T(stats)
                stat_done[oc] = pe.n

                def store(s, oc=oc, sp=sp):
                    s.wait_ge(dve.h, x2_done[oc])
                    qs.inc(s.dma_start(out=self.x2_dram.ap()[oc * 128:(oc + 1) * 128, :],
                                       in_=x2c[sp][:]), 16)
                self.S(store)
                store_done[oc] = qs.n

            def ln2_v(v):
                v.wait_ge(pe.h, stat_done[DC - 1])
                v.tensor_scalar(rstd_rep[:], msq_ps[:], 1.0 / D, None, ALU.mult)
                v.tensor_scalar(negmr_rep[:], mean_ps[:], 1.0 / D, None, ALU.mult)
                v.tensor_tensor(tchunk[:], negmr_rep[:], negmr_rep[:], ALU.mult)
                dve.inc(v.tensor_tensor(rstd_rep[:], rstd_rep[:], tchunk[:], ALU.subtract))
            self.V(ln2_v)
            var_done = dve.n

            def ln2_a(sc, var_done=var_done):
                sc.wait_ge(dve.h, var_done)
                act.inc(sc.activation(tchunk[:], rstd_rep[:], AF.Ln,
                                      bias=self.smalls[:, self.C_EPS:self.C_EPS + 1]))
                sc.wait_ge(act.h, act.n)
                act.inc(sc.activation(rstd_rep[:], tchunk[:], AF.Exp, scale=-0.5))
            self.A(ln2_a)
            rstd_done = act.n

            def ln2_v2(v, rstd_done=rstd_done):
                v.wait_ge(act.h, rstd_done)
                v.tensor_scalar(tchunk[:], negmr_rep[:], -1.0, None, ALU.mult)
                dve.inc(v.tensor_tensor(negmr_rep[:], tchunk[:], rstd_rep[:], ALU.mult))
            self.V(ln2_v2)
            nmr_done = dve.n

            x2_loaded = {}

            def x2l0(s, nmr_done=nmr_done):
                s.wait_ge(dve.h, nmr_done)
                for b in range(2):
                    qs.inc(s.dma_start(out=x2c[b][:],
                                       in_=self.x2_dram.ap()[b * 128:(b + 1) * 128, :]), 16)
                    x2_loaded[b] = qs.n
            self.S(x2l0)

            norm_done = {}
            for dc in range(DC):
                b = dc % 2

                def normc(v, dc=dc, b=b):
                    v.wait_ge(qs.h, x2_loaded[dc])
                    v.tensor_tensor(tchunk[:], x2c[b][:], rstd_rep[:], ALU.mult)
                    dve.inc(v.tensor_tensor(self.big2[:, dc * TOK:(dc + 1) * TOK],
                                            tchunk[:], negmr_rep[:], ALU.add))
                self.V(normc)
                norm_done[dc] = dve.n

                if dc + 2 < DC:
                    def x2l(s, dc=dc, b=b):
                        s.wait_ge(dve.h, norm_done[dc])
                        qs.inc(s.dma_start(out=x2c[b][:],
                                           in_=self.x2_dram.ap()[(dc + 2) * 128:(dc + 3) * 128, :]), 16)
                    self.S(x2l)
                    x2_loaded[dc + 2] = qs.n

            self.xn2T_done = dve.n

            def fence(t):
                t.wait_ge(dve.h, self.xn2T_done)
                t.wait_ge(qs.h, qs.n)
                t.wait_ge(qa.h, qa.n)
            self.T(fence)
        self._dump("x2", self.x2_dram.ap())
        self._dump("xn2T", self.big2[:])

    def p5_ffn(self):
        nc = self.nc
        qs, qa, qg, pe, act, dve = self.qs, self.qa, self.qg, self.pe, self.act, self.dve
        with contextlib.ExitStack() as _st:
            rt_ps = _st.enter_context(nc.psum_tensor("rt_ps", [8, TOK], F32))
            aux_ps = _st.enter_context(nc.psum_tensor("aux_ps", [128, TOK], F32))
            f_ps0 = _st.enter_context(nc.psum_tensor("f_ps0", [128, TOK], F32))
            f_ps1 = _st.enter_context(nc.psum_tensor("f_ps1", [128, TOK], F32))
            gwt_sb = _st.enter_context(nc.sbuf_tensor("gwt_sb", [128, DC * NE], BF16))
            ics_sb = _st.enter_context(nc.sbuf_tensor("ics_sb", [8, D], BF16))
            ocs_sb = _st.enter_context(nc.sbuf_tensor("ocs_sb", [8, D], BF16))
            et_sb = _st.enter_context(nc.sbuf_tensor("et_sb", [8, TOK], F32))
            rs_rep = _st.enter_context(nc.sbuf_tensor("rs_rep", [128, TOK], F32))
            r_sb = _st.enter_context(nc.sbuf_tensor("r_sb", [8, TOK], BF16))
            bias_cols = _st.enter_context(nc.sbuf_tensor("bias_cols", [128, DC], F32))
            bias32_sb = _st.enter_context(nc.sbuf_tensor("bias32_sb", [32, 128], F32))
            wf32 = _st.enter_context(nc.sbuf_tensor("wf32", [128, D], F32))
            swf0, swf1 = self.wbuf
            osc_sb0, osc_sb1 = self.stage
            x2c0b = _st.enter_context(nc.sbuf_tensor("x2c0b", [128, TOK], BF16))
            x2c1b = _st.enter_context(nc.sbuf_tensor("x2c1b", [128, TOK], BF16))
            tch = _st.enter_context(nc.sbuf_tensor("tch", [128, TOK], F32))
            out0 = _st.enter_context(nc.sbuf_tensor("out0", [128, TOK], F32))
            out1 = out0
            swf = [swf0, swf1]
            osc_sb = [osc_sb0, osc_sb1]
            x2c = [x2c0b, x2c1b]
            outb = [out0, out1]
            f_ps = [f_ps0, f_ps1]

            def sl(gp):
                gp.wait_ge(dve.h, self.xn2T_done)
                qg.inc(gp.dma_start(out=gwt_sb[:], in_=self.gwt_h), 16)
                qg.inc(gp.dma_start(out=ics_sb[:], in_=self.ics8), 16)
                qg.inc(gp.dma_start(out=ocs_sb[:], in_=self.ocs8), 16)
            self.G(sl)
            small_loaded = qg.n

            def bl(s):
                s.wait_ge(dve.h, self.xn2T_done)
                qs.inc(s.dma_start(out=bias32_sb[:], in_=self.bias32), 16)
            self.S(bl)
            bias_loaded = qs.n

            def btr(t):
                t.wait_ge(qs.h, bias_loaded)
                t.wait_ge(dve.h, self.xn2T_done)
                pe.inc(t.transpose(aux_ps[0:128, 0:32], bias32_sb[:],
                                   self.ident_f_sb[0:32, 0:32]))
            self.T(btr)
            btr_done = pe.n

            def bev(sc, btr_done=btr_done):
                sc.wait_ge(pe.h, btr_done)
                act.inc(sc.activation(bias_cols[:], aux_ps[0:128, 0:32], AF.Copy))
            self.A(bev)
            bias_done = act.n

            def rmm(t):
                t.wait_ge(qg.h, small_loaded)
                t.wait_ge(act.h, bias_done)
                ins = None
                for dc in range(DC):
                    for hf in range(2):
                        ins = t.matmul(rt_ps[:, hf * 512:(hf + 1) * 512],
                                       gwt_sb[:, dc * NE:(dc + 1) * NE],
                                       self.big2[:, dc * TOK + hf * 512:dc * TOK + (hf + 1) * 512],
                                       start=(dc == 0), stop=(dc == DC - 1))
                pe.inc(ins)
            self.T(rmm)
            rmm_done = pe.n

            def rexp(sc, rmm_done=rmm_done):
                sc.wait_ge(pe.h, rmm_done)
                act.inc(sc.activation(et_sb[:], rt_ps[:], AF.Exp))
            self.A(rexp)
            rexp_done = act.n

            def rsum(t, rexp_done=rexp_done):
                t.wait_ge(act.h, rexp_done)
                ins = None
                for hf in range(2):
                    ins = t.matmul(aux_ps[:, hf * 512:(hf + 1) * 512], self.ones_f_sb[0:8, :],
                                   et_sb[:, hf * 512:(hf + 1) * 512], start=True, stop=True)
                pe.inc(ins)
            self.T(rsum)
            rsum_done = pe.n

            def rrec(sc, rsum_done=rsum_done):
                sc.wait_ge(pe.h, rsum_done)
                act.inc(sc.activation(rs_rep[:], aux_ps[:], AF.Ln))
                sc.wait_ge(act.h, act.n)
                act.inc(sc.activation(rs_rep[:], rs_rep[:], AF.Exp, scale=-1.0))
            self.A(rrec)
            rrec_done = act.n

            def rnorm(v, rrec_done=rrec_done):
                v.wait_ge(act.h, rrec_done)
                dve.inc(v.tensor_tensor(r_sb[:], et_sb[:], rs_rep[0:8, :], ALU.mult))
            self.V(rnorm)
            r_done = dve.n

            isc_mm, z_done = {}, {}
            for dc in range(DC):
                bank = dc % 2

                def iscmm(t, dc=dc, bank=bank):
                    if dc == 0:
                        t.wait_ge(dve.h, r_done)
                    if dc >= 2:
                        t.wait_ge(dve.h, z_done[dc - 2])
                    ins = None
                    for hf in range(2):
                        ins = t.matmul(f_ps[bank][:, hf * 512:(hf + 1) * 512],
                                       ics_sb[:, dc * 128:(dc + 1) * 128],
                                       r_sb[:, hf * 512:(hf + 1) * 512], start=True, stop=True)
                    pe.inc(ins)
                self.T(iscmm)
                isc_mm[dc] = pe.n

                def zmul(v, dc=dc, bank=bank):
                    v.wait_ge(pe.h, isc_mm[dc])
                    dve.inc(v.tensor_tensor(self.big1[:, dc * TOK:(dc + 1) * TOK],
                                            f_ps[bank][:], self.big2[:, dc * TOK:(dc + 1) * TOK],
                                            ALU.mult))
                self.V(zmul)
                z_done[dc] = dve.n
            zT_done = dve.n

            wf_loaded, sign_done, osc_done = {}, {}, {}
            fmm_done, ep1_done, ep2_done, ep3_done = {}, {}, {}, {}
            x2_loaded, ystore = {}, {}

            def x2first(s):
                s.wait_ge(dve.h, zT_done)
                for b in range(2):
                    qs.inc(s.dma_start(out=x2c[b][:],
                                       in_=self.x2_dram.ap()[b * 128:(b + 1) * 128, :]), 16)
                    x2_loaded[b] = qs.n
            self.S(x2first)

            for oc in range(DC):
                wp, sp = oc % 2, oc % 2

                def wfl(gp, oc=oc):
                    if oc == 0:
                        gp.wait_ge(dve.h, self.xn2T_done)
                    if oc >= 1:
                        gp.wait_ge(act.h, sign_done[oc - 1])
                    qg.inc(gp.dma_start(out=wf32[:],
                                        in_=self.w_blk["wf"][oc].rearrange("p dc e -> p (dc e)")), 16)
                self.G(wfl)
                wf_loaded[oc] = qg.n

                def sgn(sc, oc=oc, wp=wp):
                    sc.wait_ge(qg.h, wf_loaded[oc])
                    if oc >= 2:
                        sc.wait_ge(pe.h, fmm_done[oc - 2])
                    act.inc(sc.sign(swf[wp][:], wf32[:]))
                self.A(sgn)
                sign_done[oc] = act.n

                def oscmm(t, oc=oc):
                    if oc == 0:
                        t.wait_ge(dve.h, zT_done)
                    if oc >= 2:
                        t.wait_ge(act.h, osc_done[oc - 2])
                    if oc >= 1:
                        t.wait_ge(act.h, osc_done[oc - 1])
                    ins = None
                    for hf in range(2):
                        ins = t.matmul(aux_ps[:, hf * 512:(hf + 1) * 512],
                                       ocs_sb[:, oc * 128:(oc + 1) * 128],
                                       r_sb[:, hf * 512:(hf + 1) * 512], start=True, stop=True)
                    pe.inc(ins)
                self.T(oscmm)
                osc_mm = pe.n

                def oscev(sc, oc=oc, sp=sp, osc_mm=osc_mm):
                    sc.wait_ge(pe.h, osc_mm)
                    if oc >= 2:
                        sc.wait_ge(dve.h, ep1_done[oc - 2])
                    act.inc(sc.activation(osc_sb[sp][:], aux_ps[:], AF.Copy))
                self.A(oscev)
                osc_done[oc] = act.n

                def fmm(t, oc=oc, wp=wp, sp=sp):
                    t.wait_ge(act.h, sign_done[oc])
                    if oc == 0:
                        t.wait_ge(dve.h, zT_done)
                    if oc >= 2:
                        t.wait_ge(dve.h, ep1_done[oc - 2])
                    ins = None
                    for hc in range(DC):
                        for hf in range(2):
                            ins = t.matmul(f_ps[sp][:, hf * 512:(hf + 1) * 512],
                                           swf[wp][:, hc * 128:(hc + 1) * 128],
                                           self.big1[:, hc * TOK + hf * 512:hc * TOK + (hf + 1) * 512],
                                           start=(hc == 0), stop=(hc == DC - 1))
                    pe.inc(ins)
                self.T(fmm)
                fmm_done[oc] = pe.n

                def ep1(v, oc=oc, sp=sp):
                    v.wait_ge(pe.h, fmm_done[oc])
                    v.wait_ge(act.h, osc_done[oc])
                    if oc >= 1:
                        v.wait_ge(dve.h, ep3_done[oc - 1])
                    dve.inc(v.tensor_tensor(tch[:], f_ps[sp][:], osc_sb[sp][:], ALU.mult))
                self.V(ep1)
                ep1_done[oc] = dve.n

                def ep3(v, oc=oc, sp=sp):
                    v.wait_ge(qs.h, x2_loaded[oc])
                    if oc >= 1:
                        v.wait_ge(qs.h, ystore[oc - 1])
                    dve.inc(v.scalar_tensor_tensor(outb[sp][:], tch[:],
                                                   bias_cols[:, oc:oc + 1], x2c[sp][:],
                                                   ALU.add, ALU.add))
                self.V(ep3)
                ep3_done[oc] = dve.n

                def yst(s, oc=oc, sp=sp):
                    s.wait_ge(dve.h, ep3_done[oc])
                    qs.inc(s.dma_start(out=self.y_fm[oc * 128:(oc + 1) * 128, :],
                                       in_=outb[sp][:]), 16)
                    if oc + 2 < DC:
                        qs.inc(s.dma_start(out=x2c[sp][:],
                                           in_=self.x2_dram.ap()[(oc + 2) * 128:(oc + 3) * 128, :]), 16)
                self.S(yst)
                ystore[oc] = qs.n - (16 if oc + 2 < DC else 0)
                if oc + 2 < DC:
                    x2_loaded[oc + 2] = qs.n

            self._dump("zT", self.big1[:])
            self._dump("r8", r_sb[:])
            self._dump("et8", et_sb[:])
            self._dump("rsrep", rs_rep[:])
            self._dump("bias_cols", bias_cols[:])
            self._dump("osc_last", osc_sb[(DC - 1) % 2][:])
            self._dump("tch_last", tch[:])

    def finalize(self):
        qs, qa, qg = self.qs, self.qa, self.qg

        def fin(s):
            s.wait_ge(qa.h, qa.n)
            s.wait_ge(qg.h, qg.n)
            s.wait_ge(qs.h, qs.n)
        self.S(fin)

        def fing(gp):
            gp.wait_ge(qs.h, qs.n)
        self.G(fing)


def build_nc(debug_dumps=(), phases=5):
    return Builder(debug_dumps, phases).build()


def _prep_inputs(L0_attn_x, L0_attn_w_q, L0_attn_w_k, L0_attn_w_v, L0_attn_w_o,
                 L0_ffn_weight, L0_ffn_bias, L0_ffn_gate_w,
                 L0_ffn_in_channel_scale, L0_ffn_out_channel_scale):
    x = np.ascontiguousarray(np.asarray(L0_attn_x, dtype=np.float32))
    wq = np.asarray(L0_attn_w_q, dtype=np.float32)
    wk = np.asarray(L0_attn_w_k, dtype=np.float32)
    wv = np.asarray(L0_attn_w_v, dtype=np.float32)
    wo = np.asarray(L0_attn_w_o, dtype=np.float32)
    wf = np.asarray(L0_ffn_weight, dtype=np.float32)
    gw = np.asarray(L0_ffn_gate_w, dtype=np.float32)
    bias = np.asarray(L0_ffn_bias, dtype=np.float32)
    ics = np.ascontiguousarray(np.asarray(L0_ffn_in_channel_scale, dtype=np.float32))
    ocs = np.ascontiguousarray(np.asarray(L0_ffn_out_channel_scale, dtype=np.float32))

    shared = {
        "wq_blk": _blocked_T(wq), "wk_blk": _blocked_T(wk), "wv_blk": _blocked_T(wv),
        "wo_blk": _blocked_T(wo), "wf_blk": _blocked_T(wf),
        "gwt_h": np.ascontiguousarray(
            gw.T.reshape(DC, 128, NE).transpose(1, 0, 2).reshape(128, DC * NE)),
        "bias32": np.ascontiguousarray(bias.reshape(32, 128)),
        "ics8": ics, "ocs8": ocs,
    }
    xf = x.reshape(2 * 4096, D)
    in_maps = []
    for c in range(NCORE):
        m = dict(shared)
        m["x_t"] = np.ascontiguousarray(xf[c * TOK:(c + 1) * TOK, :])
        in_maps.append(m)
    return in_maps


_NC_CACHE = {}
PHASES = 5


def kernel(**inputs):
    in_maps = _prep_inputs(**inputs)
    key = (tuple(DEBUG_DUMPS), PHASES)
    if key not in _NC_CACHE:
        _NC_CACHE[key] = build_nc(key[0], PHASES)
    nc = _NC_CACHE[key]
    res = run_bass_kernel_spmd(nc, in_maps, core_ids=list(range(NCORE)))
    outs = [np.ascontiguousarray(np.asarray(res.results[c]["y_fm"]).T) for c in range(NCORE)]
    full = np.concatenate(outs, axis=0).reshape(2, 4096, D)
    if DEBUG_DUMPS:
        kernel.last_results = res
    return full
